# version 08
# speedup vs baseline: 1.3874x; 1.0005x over previous
import os
import numpy as np
from contextlib import ExitStack
import concourse.bass as bass
import concourse.mybir as mybir
from concourse.bass_utils import run_bass_kernel_spmd

F32 = mybir.dt.float32
BF16 = mybir.dt.bfloat16
U8 = mybir.dt.uint8
AF = mybir.ActivationFunctionType
ALU = mybir.AluOpType
AX = mybir.AxisListType

D = 1024
T = 2048
NH = 16
HD = 64
LAT = 256
HID = 4096
NC = 8
TQB = 512
NB = T // TQB
HALF = T // 2
NBH = HALF // TQB
NTK = T // 128
FT = 128
NFC = T // FT
R23 = float(1.5 * 2.0 ** 23)

ENGINES = ("sync", "tensor", "vector", "scalar", "gpsimd")
_DTS = {F32: 4, BF16: 2, U8: 1}


class Arena:

    def __init__(self, t):
        self.t = t
        self.cur = 0
        self.marks = []
        self.hi = 0

    def alloc(self, dtype, parts, cols):
        off = (self.cur + 15) & ~15
        nbytes = cols * _DTS[dtype]
        self.cur = off + nbytes
        self.hi = max(self.hi, self.cur)
        return self.t[0:parts, off:off + nbytes].bitcast(dtype)

    def mark(self):
        self.marks.append(self.cur)

    def release(self):
        self.cur = self.marks.pop()


class Prog:

    def __init__(self, nc):
        self.nc = nc
        self.streams = {e: [] for e in ENGINES}
        self.sems = {}
        self.mkr_v = None
        self.mkr_s = None
        self.remap_gpsimd = True

    def new_sem(self, stack, name):
        h = stack.enter_context(self.nc.semaphore(name))
        self.sems[name] = [h, 0]

    def val(self, name):
        return self.sems[name][1]

    def bump(self, name, n=1):
        self.sems[name][1] += n
        return self.sems[name][1]

    def em(self, eng, fn):
        self.streams[eng].append(fn)

    def _q(self, eng):
        return "sync" if eng == "gpsimd" and self.remap_gpsimd else eng

    def wait(self, eng, name, v=None):
        h, cur = self.sems[name]
        if v is None:
            v = cur
        assert v <= cur, f"wait on future value of {name}: {v} > {cur}"
        if v > 0:
            self.em(self._q(eng), lambda e, h=h, v=v: e.wait_ge(h, v))

    def op(self, eng, make, inc=None, n=1):
        eng = self._q(eng)
        if inc is None:
            self.em(eng, lambda e, make=make: make(e))
            return None
        h = self.sems[inc][0]
        v = self.bump(inc, n)
        self.em(eng, lambda e, make=make, h=h, n=n: make(e).then_inc(h, n))
        return v

    def marker(self, eng, sem):
        h = self.sems[sem][0]
        v = self.bump(sem, 1)
        if eng == "vector" and self.mkr_v is not None:
            d, s = self.mkr_v
            self.em(eng, lambda e, h=h, d=d, s=s:
                    e.tensor_copy(d, s).then_inc(h, 1))
        elif eng == "scalar" and self.mkr_s is not None:
            d, s = self.mkr_s
            self.em(eng, lambda e, h=h, d=d, s=s:
                    e.memzero(d).then_inc(h, 1))
        else:
            self.em(eng, lambda e, h=h: e.nop().then_inc(h, 1))
        return v

    def barrier(self):
        for s in ("sLD", "sLDb", "sST", "sSTb"):
            self.wait("sync", s)
        for eng in ENGINES:
            self.marker(eng, "sBAR")
        tgt = self.val("sBAR")
        h = self.sems["sBAR"][0]
        for eng in ENGINES:
            self.em(eng, lambda e, h=h, v=tgt: e.wait_ge(h, v))


def build_program(dbg=()):
    _bs = os.environ.get("MK_STAGE", "")
    _npair = int(os.environ.get("MK_NPAIR", "8"))
    _nhalf = int(os.environ.get("MK_NHALF", "2"))
    _nfc = int(os.environ.get("MK_NFC", str(NFC)))
    _p3 = "nop3" not in _bs
    _p4 = "nop4" not in _bs
    _p5 = "nop5" not in _bs
    nc = bass.Bass("TRN2", target_bir_lowering=False, debug=False,
                   detect_race_conditions=False)
    din = {}
    for name, shape in [
        ("xT", [D, T]), ("wqT", [D, D]), ("wkaT", [D, LAT]), ("wkbT", [LAT, D]),
        ("wvaT", [D, LAT]), ("wvbT", [LAT, D]), ("woT", [D, D]),
        ("w1T", [D, HID]), ("w2T", [HID, D]),
        ("convw", [HID, 3]), ("convb", [HID, 1]),
        ("ropeC", [128, 1024]), ("ropeS", [128, 1024]),
        ("ones_col", [128, 1]), ("ones_row", [1, 128]), ("e2", [128, 2]),
        ("shift64", [64, 128]), ("ident", [128, 128]),
    ]:
        din[name] = nc.dram_tensor(name, shape, F32, kind="ExternalInput").ap()
    y = nc.dram_tensor("y", [D, T], F32, kind="ExternalOutput").ap()
    scx = nc.dram_tensor("scx", [D, T], F32, kind="Internal").ap()
    scxn = nc.dram_tensor("scxn", [D, T], F32, kind="Internal").ap()
    scx2 = nc.dram_tensor("scx2", [D, T], F32, kind="Internal").ap()
    dbg_out = {}

    p = Prog(nc)
    with ExitStack() as S:
        for s in ("sLD", "sLDb", "sST", "sSTb", "sPE", "sVE", "sSE", "sGP", "sSY", "sBAR"):
            p.new_sem(S, s)
        pbT = [S.enter_context(nc.psum_tensor(f"pbT{i}", [128, 1024], F32))
               for i in range(4)]
        pb = [pbT[i // 2][:, (i % 2) * 512:(i % 2 + 1) * 512] for i in range(8)]
        A = Arena(S.enter_context(nc.sbuf_tensor("arena", [128, 210432], U8)))

        ident = A.alloc(F32, 128, 128)
        ones_col = A.alloc(F32, 128, 1)
        ones_row = A.alloc(F32, 1, 128)
        e2sb = A.alloc(F32, 128, 2)
        shift64 = A.alloc(F32, 64, 128)
        nrm = A.alloc(F32, 2, 16)
        negb = A.alloc(F32, 128, 2)
        wrow = A.alloc(F32, 1, 8)
        wstat = A.alloc(F32, 128, 8)
        convw_sb = A.alloc(F32, 128, 96)
        convb_sb = A.alloc(F32, 128, 32)
        rowA = A.alloc(F32, 1, 3600)
        rowB = A.alloc(F32, 1, 2816)
        absb = A.alloc(F32, 128, 2 * 512)
        sqs = A.alloc(F32, 128, 2 * 512)
        tts = A.alloc(F32, 128, 512)

        p.mkr_v = (rowB[0:1, 2752:2760], rowB[0:1, 2744:2752])
        p.mkr_s = (rowB[0:1, 2736:2744], rowB[0:1, 2728:2736])
        p.op("vector", lambda e: e.memset(rowB[0:1, 2744:2760], 0.0))
        p.op("scalar", lambda e: e.memzero(rowB[0:1, 2728:2744]))

        def vspace():
            p.op("vector", lambda e: e.tensor_copy(rowB[0:1, 2784:2800],
                                                   rowB[0:1, 2768:2784]))

        def ldrow(dst_ap, src_ap, sem="sLD"):
            return p.op("sync", lambda e, d=dst_ap, s=src_ap:
                        e.dma_start(out=d, in_=s), inc=sem, n=16)

        def tap(name, ap, shape):
            if name not in dbg:
                return
            t = nc.dram_tensor("dbg_" + name, list(shape), F32,
                               kind="ExternalOutput").ap()
            dbg_out[name] = t
            p.barrier()
            p.op("gpsimd", lambda e, t=t, ap=ap: e.dma_start(out=t, in_=ap),
                 inc="sST", n=16)
            p.barrier()

        ldrow(ident, din["ident"])
        ldrow(ones_col, din["ones_col"])
        ldrow(ones_row, din["ones_row"])
        ldrow(e2sb, din["e2"])
        ldrow(shift64, din["shift64"])
        for hc in range(32):
            ldrow(convw_sb[:, hc * 3:(hc + 1) * 3],
                  din["convw"][hc * 128:(hc + 1) * 128, :])
            ldrow(convb_sb[:, hc:hc + 1], din["convb"][hc * 128:(hc + 1) * 128, :])

        def ln_stats(get_chunk, npos):
            msq_v = [0] * NC
            for c in range(NC):
                if c >= 2:
                    p.wait("scalar", "sPE", msq_v[c - 2])
                sv = p.op("scalar", lambda e, c=c, npos=npos: e.activation(
                    sqs[:, (c % 2) * 512:(c % 2) * 512 + npos], get_chunk(c),
                    AF.Square), inc="sSE")
                p.op("tensor", lambda e, c=c, npos=npos: nc.tensor.matmul(
                    pbT[0][0:1, 0:npos], ones_col, get_chunk(c),
                    start=(c == 0), stop=(c == NC - 1)), inc="sPE")
                p.wait("tensor", "sSE", sv)
                msq_v[c] = p.op("tensor", lambda e, c=c, npos=npos: nc.tensor.matmul(
                    pbT[0][0:1, 512:512 + npos], ones_col,
                    sqs[:, (c % 2) * 512:(c % 2) * 512 + npos],
                    start=(c == 0), stop=(c == NC - 1)), inc="sPE")
            p.wait("vector", "sPE")
            p.wait("scalar", "sPE")
            cut = float(os.environ.get("MK_LNCUT", "99"))
            if cut < 2:
                return p.marker("vector", "sVE")
            mu = rowA[0:1, 0:npos]
            e2x = rowA[0:1, 512:512 + npos]
            var = rowA[0:1, 1024:1024 + npos]
            ve = rowA[0:1, 1536:1536 + npos]
            rr = rowA[0:1, 2048:2048 + npos]
            bb = rowA[0:1, 2560:2560 + npos]
            t0 = rowB[0:1, 0:npos]
            raw = rowB[0:1, 1280:2304]
            p.op("vector", lambda e: e.tensor_copy(raw, pbT[0][0:1, 0:1024]))
            if cut < 3:
                return p.marker("vector", "sVE")
            p.op("vector", lambda e, npos=npos: e.tensor_scalar(
                mu, raw[0:1, 0:npos], 1.0 / D, None, ALU.mult))
            p.op("vector", lambda e, npos=npos: e.tensor_scalar(
                e2x, raw[0:1, 512:512 + npos], 1.0 / D, None, ALU.mult))
            p.op("vector", lambda e: e.tensor_mul(var, mu, mu))
            vspace()
            p.op("vector", lambda e: e.tensor_sub(var, e2x, var))
            vspace()
            vv = p.op("vector", lambda e: e.tensor_scalar(
                ve, var, 1e-5, None, ALU.add), inc="sVE")
            if cut < 4:
                return vv
            p.wait("scalar", "sVE", vv)
            sv = p.op("scalar", lambda e: e.activation(rr, ve, AF.Sqrt), inc="sSE")
            p.wait("vector", "sSE", sv)
            p.op("vector", lambda e: e.reciprocal(rr, rr))
            vspace()
            p.op("vector", lambda e: e.tensor_mul(t0, rr, rr))
            vspace()
            p.op("vector", lambda e: e.tensor_mul(t0, t0, ve))
            vspace()
            p.op("vector", lambda e: e.tensor_scalar(t0, t0, -0.5, 1.5,
                                                     ALU.mult, ALU.add))
            vspace()
            p.op("vector", lambda e: e.tensor_mul(rr, rr, t0))
            vspace()
            vv = p.op("vector", lambda e: e.scalar_tensor_tensor(
                bb, mu, -1.0, rr, ALU.mult, ALU.mult), inc="sVE")
            if cut < 5:
                return vv
            p.wait("tensor", "sVE", vv)
            p.op("tensor", lambda e, npos=npos: nc.tensor.matmul(
                pbT[1][:, 0:npos], ones_row, rr, start=True, stop=True), inc="sPE")
            if cut < 45.5:
                return p.marker("vector", "sVE")
            pv = p.op("tensor", lambda e, npos=npos: nc.tensor.matmul(
                pbT[1][:, 512:512 + npos], ones_row, bb,
                start=True, stop=True), inc="sPE")
            if cut < 6:
                return p.marker("vector", "sVE")
            p.wait("vector", "sPE", pv)
            return p.op("vector", lambda e: e.tensor_copy(
                absb[:, 0:1024], pbT[1][:, 0:1024]), inc="sVE")

        A.mark()
        kcsb = A.alloc(F32, 128, 2 * T)
        kvsb = A.alloc(F32, 128, 2 * T)
        wvbT_sb = A.alloc(F32, 128, 2 * D)
        ropec = A.alloc(F32, 128, 1024)
        ropes = A.alloc(F32, 128, 1024)
        A.mark()
        xblk = A.alloc(F32, 128, NC * TQB)
        wka_sb = A.alloc(F32, 128, NC * LAT)
        wva_sb = A.alloc(F32, 128, NC * LAT)
        for c in range(NC):
            ldrow(wka_sb[:, c * LAT:(c + 1) * LAT],
                  din["wkaT"][c * 128:(c + 1) * 128, :])
            ldrow(wva_sb[:, c * LAT:(c + 1) * LAT],
                  din["wvaT"][c * 128:(c + 1) * 128, :])
        p.barrier()
        for blk in range(NB):
            base = blk * TQB

            def xc(c):
                return xblk[:, c * TQB:(c + 1) * TQB]
            lds = [ldrow(xc(c), din["xT"][c * 128:(c + 1) * 128, base:base + TQB])
                   for c in range(NC)]
            if blk == 0:
                tap("x0", xblk[:, 0:NC * TQB], (128, NC * TQB))
            p.wait("gpsimd", "sLD", lds[-1])
            for c in range(NC):
                p.op("gpsimd", lambda e, c=c, base=base: e.dma_start(
                    out=scx[c * 128:(c + 1) * 128, base:base + TQB], in_=xc(c)),
                    inc="sST", n=16)
            xspill = p.val("sST")
            p.wait("gpsimd", "sST", xspill)
            p.wait("tensor", "sLD", lds[-1])
            p.wait("scalar", "sLD", lds[-1])
            vv = ln_stats(xc, TQB)
            if blk == 0:
                tap("ab0", absb[:, 0:1024], (128, 1024))
            p.wait("vector", "sVE", vv)
            p.wait("vector", "sST", xspill)
            for c in range(NC):
                p.op("vector", lambda e, c=c: e.tensor_mul(
                    tts[:, 0:TQB], xc(c), absb[:, 0:TQB]))
                p.op("vector", lambda e, c=c: e.tensor_add(
                    xc(c), tts[:, 0:TQB], absb[:, 512:1024]), inc="sVE")
            xn_done = p.val("sVE")
            if blk == 0:
                tap("xn0", xblk[:, 0:NC * TQB], (128, NC * TQB))
                if "xncpy" in dbg:
                    for c in range(NC):
                        p.op("vector", lambda e, c=c: e.tensor_copy(
                            kvsb[:, c * TQB:(c + 1) * TQB], xc(c)), inc="sVE")
                    tap("xncpy", kvsb[:, 0:NC * TQB], (128, NC * TQB))
            for c in range(NC):
                p.wait("gpsimd", "sVE", xn_done)
                p.op("gpsimd", lambda e, c=c, base=base: e.dma_start(
                    out=scxn[c * 128:(c + 1) * 128, base:base + TQB], in_=xc(c)),
                    inc="sST", n=16)
            p.wait("tensor", "sVE", xn_done)
            for ti, (wsb, dst) in enumerate(((wka_sb, kcsb), (wva_sb, kvsb))):
                for lc in range(2):
                    bank = pb[(ti * 2 + lc) % 4 + 4]
                    for c in range(NC):
                        p.op("tensor", lambda e, wsb=wsb, lc=lc, c=c, bank=bank:
                             nc.tensor.matmul(
                                 bank[:, :],
                                 wsb[:, c * LAT + lc * 128: c * LAT + (lc + 1) * 128],
                                 xc(c), start=(c == 0), stop=(c == NC - 1)),
                             inc="sPE")
                    p.wait("vector", "sPE")
                    p.op("vector", lambda e, dst=dst, lc=lc, base=base, bank=bank:
                         e.tensor_copy(
                             dst[:, lc * T + base: lc * T + base + TQB], bank[:, :]),
                         inc="sVE")
            p.wait("tensor", "sVE")
            p.barrier()
        tap("kc", kcsb[:, 0:2 * T], (128, 2 * T))
        tap("kv", kvsb[:, 0:2 * T], (128, 2 * T))
        A.release()

        if "nowl" not in _bs:
            for lc in range(2):
                ldrow(wvbT_sb[:, lc * D:(lc + 1) * D],
                      din["wvbT"][lc * 128:(lc + 1) * 128, :])
            ldrow(ropec, din["ropeC"])
            ldrow(ropes, din["ropeS"])
        A.mark()
        xnh = A.alloc(F32, 128, NC * HALF)
        aoh = A.alloc(F32, 128, NC * HALF)
        wq_sl = A.alloc(F32, 128, 2 * NC * 128)
        wk_sl = A.alloc(F32, 128, 2 * 2 * 128)
        wo_sl = A.alloc(F32, 128, 2 * D)
        q_h = A.alloc(F32, 128, HALF)
        k_pair = A.alloc(F32, 128, T)
        v_pad = A.alloc(F32, 128, 2 * 16 * 65)
        attnT = A.alloc(F32, 128, 2 * TQB)
        oT_h = A.alloc(F32, 128, HALF)
        tmpB = A.alloc(F32, 64, TQB)
        rs_bc = A.alloc(F32, 64, TQB)

        def rope_cols(wsl, nchunks, pair):
            for c in range(nchunks):
                for j in range(2):
                    o = c * 128 + 64 * j
                    E = wsl[:, o:o + 16:2]
                    O = wsl[:, o + 1:o + 17:2]
                    rc_ = ropec[:, pair * 128 + j * 8: pair * 128 + (j + 1) * 8]
                    rs_ = ropes[:, pair * 128 + j * 8: pair * 128 + (j + 1) * 8]
                    t1, t2 = tts[:, 0:8], tts[:, 8:16]
                    t3, t4 = tts[:, 16:24], tts[:, 24:32]
                    p.op("vector", lambda e, E=E, rc_=rc_, t1=t1: e.tensor_mul(t1, E, rc_))
                    p.op("vector", lambda e, O=O, rs_=rs_, t2=t2: e.tensor_mul(t2, O, rs_))
                    p.op("vector", lambda e, E=E, rs_=rs_, t3=t3: e.tensor_mul(t3, E, rs_))
                    p.op("vector", lambda e, O=O, rc_=rc_, t4=t4: e.tensor_mul(t4, O, rc_))
                    p.op("vector", lambda e, E=E, t1=t1, t2=t2: e.tensor_sub(E, t1, t2))
                    p.op("vector", lambda e, O=O, t3=t3, t4=t4: e.tensor_add(O, t3, t4),
                         inc="sVE")

        for _i in range(int(os.environ.get("MK_DUMDMA", "0"))):
            ldrow(ident, din["ident"])
        if "dumb" in _bs:
            p.barrier()
            p.barrier()
        if "dumw" in _bs:
            for c in range(NC):
                ldrow(wq_sl[:, c * 128:(c + 1) * 128],
                      din["wqT"][c * 128:(c + 1) * 128, 0:128])
            for lc in range(2):
                ldrow(wk_sl[:, lc * 128:(lc + 1) * 128],
                      din["wkbT"][lc * 128:(lc + 1) * 128, 0:128])
            ldrow(wo_sl[:, 0:D], din["woT"][0:128, :])
        if "nomemset" not in _bs:
            p.op("vector", lambda e: e.memset(nrm[:, :], 0.0))
        vm = p.op("vector", lambda e: e.memset(
            v_pad[:, 64:65] if "nomemset" in _bs else v_pad[:, 64:32 * 65:65],
            1.0), inc="sVE")

        def wviews(par):
            return (wq_sl[:, par * NC * 128:(par + 1) * NC * 128],
                    wk_sl[:, par * 256:(par + 1) * 256],
                    wo_sl[:, par * D:(par + 1) * D])

        def load_pair_w(pr):
            wq_p, wk_p, wo_p = wviews(pr % 2)
            for c in range(NC):
                ldrow(wq_p[:, c * 128:(c + 1) * 128],
                      din["wqT"][c * 128:(c + 1) * 128,
                                 pr * 128:(pr + 1) * 128])
            for lc in range(2):
                ldrow(wk_p[:, lc * 128:(lc + 1) * 128],
                      din["wkbT"][lc * 128:(lc + 1) * 128,
                                  pr * 128:(pr + 1) * 128])
            ldrow(wo_p, din["woT"][pr * 128:(pr + 1) * 128, :])

        for half in range(_nhalf if _p3 else 0):
            hbase = half * HALF
            if "noxnl" not in _bs:
                for c in range(NC):
                    ldrow(xnh[:, c * HALF:(c + 1) * HALF],
                          scxn[c * 128:(c + 1) * 128, hbase:hbase + HALF])
            load_pair_w(0)
            p.barrier()
            roped_v = {0: 0, 1: 0}
            for pair in range(0 if "p4only" in _bs else _npair):
                hA, hB = 2 * pair, 2 * pair + 1
                par = pair % 2
                wq_p, wk_p, wo_p = wviews(par)
                pcut = float(os.environ.get("MK_PAIRCUT", "99"))
                if pair == 0:
                    if "norope" not in _bs:
                        rope_cols(wq_p, NC, pair)
                        rope_cols(wk_p, 2, pair)
                    rope_done = p.val("sVE")
                else:
                    rope_done = roped_v[par]
                if pcut < 2:
                    p.barrier()
                    continue
                p.wait("tensor", "sVE", rope_done)
                for blk in range(NBH):
                    bank = pb[blk % 2]
                    for c in range(NC):
                        p.op("tensor",
                             lambda e, blk=blk, c=c, bank=bank, wq_p=wq_p:
                             nc.tensor.matmul(
                                 bank[:, :], wq_p[:, c * 128:(c + 1) * 128],
                                 xnh[:, c * HALF + blk * TQB: c * HALF + blk * TQB + TQB],
                                 start=(c == 0), stop=(c == NC - 1)), inc="sPE")
                    p.wait("vector", "sPE")
                    p.op("vector", lambda e, blk=blk, bank=bank: e.tensor_copy(
                        q_h[:, blk * TQB:(blk + 1) * TQB], bank[:, :]), inc="sVE")
                    p.wait("tensor", "sVE")
                for blk in range(NB):
                    bank = pb[blk % 2]
                    for lc in range(2):
                        p.op("tensor",
                             lambda e, blk=blk, lc=lc, bank=bank, wk_p=wk_p:
                             nc.tensor.matmul(
                                 bank[:, :], wk_p[:, lc * 128:(lc + 1) * 128],
                                 kcsb[:, lc * T + blk * TQB: lc * T + blk * TQB + TQB],
                                 start=(lc == 0), stop=(lc == 1)), inc="sPE")
                    p.wait("vector", "sPE")
                    p.op("vector", lambda e, blk=blk, bank=bank: e.tensor_copy(
                        k_pair[:, blk * TQB:(blk + 1) * TQB], bank[:, :]), inc="sVE")
                    p.wait("tensor", "sVE")
                V_ev = [0] * NTK
                for tc in range(NTK):
                    bank = pb[tc % 4]
                    if tc >= 4:
                        p.wait("tensor", "sVE", V_ev[tc - 4])
                    vv_mm = 0
                    for lc in range(2):
                        vv_mm = p.op(
                            "tensor", lambda e, tc=tc, lc=lc, pair=pair, bank=bank:
                            nc.tensor.matmul(
                                bank[:, 0:128],
                                kvsb[:, lc * T + tc * 128: lc * T + (tc + 1) * 128],
                                wvbT_sb[:, lc * D + pair * 128:
                                        lc * D + (pair + 1) * 128],
                                start=(lc == 0), stop=(lc == 1)), inc="sPE")
                    p.wait("vector", "sPE", vv_mm)
                    p.op("vector", lambda e, tc=tc, bank=bank: e.tensor_copy(
                        v_pad[:, tc * 65: tc * 65 + 64], bank[:, 0:64]))
                    V_ev[tc] = p.op("vector", lambda e, tc=tc, bank=bank:
                                    e.tensor_copy(
                        v_pad[:, 1040 + tc * 65: 1040 + tc * 65 + 64],
                        bank[:, 64:128]), inc="sVE")
                p.wait("tensor", "sVE", V_ev[NTK - 1])
                if pair == 0 and half == 0:
                    tap("q_pair0", q_h[:, 0:HALF], (128, HALF))
                    tap("k_pair0", k_pair[:, 0:T], (128, T))
                if pcut < 3:
                    p.barrier()
                    continue
                for (src, col, nb_) in ((q_h, 0, NBH), (k_pair, 1, NB)):
                    for blk in range(nb_):
                        vv = p.op("vector", lambda e, src=src, blk=blk: e.tensor_mul(
                            tts[:, 0:TQB], src[:, blk * TQB:(blk + 1) * TQB],
                            src[:, blk * TQB:(blk + 1) * TQB]), inc="sVE")
                        p.wait("tensor", "sVE", vv)
                        pv = p.op("tensor", lambda e: nc.tensor.matmul(
                            pb[2][0:2, :], e2sb, tts[:, 0:TQB],
                            start=True, stop=True), inc="sPE")
                        p.wait("vector", "sPE", pv)
                        p.op("vector", lambda e, col=col, blk=blk: e.reduce_max(
                            nrm[0:2, col * 4 + blk: col * 4 + blk + 1],
                            pb[2][0:2, :], axis=AX.X), inc="sVE")
                p.op("vector", lambda e: e.reduce_max(
                    nrm[0:2, 8:9], nrm[0:2, 0:NBH], axis=AX.X))
                p.op("vector", lambda e: e.reduce_max(
                    nrm[0:2, 9:10], nrm[0:2, 4:8], axis=AX.X))
                vspace()
                vv = p.op("vector", lambda e: e.tensor_mul(
                    nrm[0:2, 10:11], nrm[0:2, 8:9], nrm[0:2, 9:10]), inc="sVE")
                p.wait("scalar", "sVE", vv)
                sv = p.op("scalar", lambda e: e.activation(
                    nrm[0:2, 11:12], nrm[0:2, 10:11], AF.Sqrt), inc="sSE")
                p.wait("vector", "sSE", sv)
                vv = p.op("vector", lambda e: e.tensor_scalar(
                    nrm[0:2, 12:13], nrm[0:2, 11:12], -0.125, None, ALU.mult),
                    inc="sVE")
                p.wait("tensor", "sVE", vv)
                pv = p.op("tensor", lambda e: nc.tensor.transpose(
                    pb[2][0:1, 0:2], nrm[0:2, 12:13], ident[0:2, 0:2]), inc="sPE")
                p.wait("vector", "sPE", pv)
                nbrow = rowA[0:1, 3584:3586]
                vv = p.op("vector", lambda e, nbrow=nbrow: e.tensor_copy(
                    nbrow, pb[2][0:1, 0:2]), inc="sVE")
                p.wait("tensor", "sVE", vv)
                pv = p.op("tensor", lambda e, nbrow=nbrow: nc.tensor.matmul(
                    pb[3][:, 0:2], ones_row, nbrow,
                    start=True, stop=True), inc="sPE")
                p.wait("vector", "sPE", pv)
                vspace()
                p.op("vector", lambda e: e.tensor_copy(negb[:, 0:2], pb[3][:, 0:2]),
                     inc="sVE")
                p.wait("tensor", "sVE")
                if pair + 1 < _npair and "p4only" not in _bs:
                    load_pair_w(pair + 1)
                    p.wait("vector", "sLD")
                    if "norope" not in _bs:
                        wq_n, wk_n, _ = wviews((pair + 1) % 2)
                        rope_cols(wq_n, NC, pair + 1)
                        rope_cols(wk_n, 2, pair + 1)
                    roped_v[(pair + 1) % 2] = p.val("sVE")
                if pcut < 4:
                    p.barrier()
                    continue
                for hl, head in ((0, hA), (1, hB)):
                    rows = slice(hl * 64, (hl + 1) * 64)
                    vbase = hl * 1040
                    if pair == 0 and hl == 0 and half == 0:
                        tap("v_pad0", v_pad[:, 0:1040], (128, 1040))
                    for blk in range(NBH):
                        avbank = pb[6 + (blk % 2)]
                        E_done = [0] * NTK
                        A_done = [0] * NTK
                        def av_mm(tc, avbank=avbank, vbase=vbase):
                            p.wait("tensor", "sSE", E_done[tc])
                            A_done[tc] = p.op(
                                "tensor", lambda e, tc=tc, avbank=avbank, vbase=vbase:
                                nc.tensor.matmul(
                                    avbank[0:65, :],
                                    v_pad[:, vbase + tc * 65: vbase + tc * 65 + 65],
                                    attnT[:, (tc % 2) * TQB:(tc % 2 + 1) * TQB],
                                    start=(tc == 0), stop=(tc == NTK - 1)),
                                inc="sPE")
                        for tc in range(NTK):
                            scb = pb[4 + (tc % 2)]
                            if tc >= 2:
                                p.wait("tensor", "sSE", E_done[tc - 2])
                            sc_v = p.op(
                                "tensor",
                                lambda e, tc=tc, blk=blk, rows=rows, scb=scb:
                                nc.tensor.matmul(
                                    scb[:, :],
                                    k_pair[rows, tc * 128:(tc + 1) * 128],
                                    q_h[rows, blk * TQB:(blk + 1) * TQB],
                                    start=True, stop=True), inc="sPE")
                            if tc >= 1:
                                av_mm(tc - 1)
                            p.wait("scalar", "sPE", sc_v)
                            if tc >= 2:
                                p.wait("scalar", "sPE", A_done[tc - 2])
                            E_done[tc] = p.op(
                                "scalar", lambda e, tc=tc, hl=hl, scb=scb:
                                e.activation(
                                    attnT[:, (tc % 2) * TQB:(tc % 2 + 1) * TQB],
                                    scb[:, :], AF.Exp, bias=negb[:, hl:hl + 1],
                                    scale=0.125), inc="sSE")
                        av_mm(NTK - 1)
                        p.wait("vector", "sPE", A_done[NTK - 1])
                        rs_row = rowA[0:1, 3072:3072 + TQB]
                        vspace()
                        vv = p.op("vector", lambda e, avbank=avbank: e.reciprocal(
                            rs_row, avbank[64:65, :]), inc="sVE")
                        p.wait("tensor", "sVE", vv)
                        pv = p.op("tensor", lambda e: nc.tensor.matmul(
                            pb[2][0:64, :], ones_row[0:1, 0:64], rs_row,
                            start=True, stop=True), inc="sPE")
                        p.wait("vector", "sPE", pv)
                        vspace()
                        p.op("vector", lambda e: e.tensor_copy(rs_bc, pb[2][0:64, :]))
                        vspace()
                        if hl == 0:
                            vv = p.op("vector", lambda e, blk=blk, avbank=avbank:
                                      e.tensor_mul(
                                          oT_h[0:64, blk * TQB:(blk + 1) * TQB],
                                          avbank[0:64, :], rs_bc), inc="sVE")
                            p.wait("tensor", "sVE", vv)
                        else:
                            vv = p.op("vector", lambda e, avbank=avbank: e.tensor_mul(
                                tmpB, avbank[0:64, :], rs_bc), inc="sVE")
                            p.wait("tensor", "sVE", vv)
                            pv = p.op("tensor", lambda e: nc.tensor.matmul(
                                pb[3][:, :], shift64, tmpB,
                                start=True, stop=True), inc="sPE")
                            p.wait("vector", "sPE", pv)
                            vspace()
                            p.op("vector", lambda e, blk=blk: e.tensor_copy(
                                oT_h[64:128, blk * TQB:(blk + 1) * TQB],
                                pb[3][64:128, :]), inc="sVE")
                            p.wait("tensor", "sVE")
                if pair == 0 and half == 0:
                    tap("oT_pair0", oT_h[:, 0:HALF], (128, HALF))
                if pcut < 5:
                    p.barrier()
                    continue
                wo_ev = [0] * (NC * NBH)
                for i, (dm, blk) in enumerate(
                        (dm, blk) for dm in range(NC) for blk in range(NBH)):
                    bank = pb[i % 4]
                    if i >= 4:
                        p.wait("tensor", "sVE", wo_ev[i - 4])
                    pv = p.op("tensor", lambda e, dm=dm, blk=blk, bank=bank,
                              wo_p=wo_p:
                              nc.tensor.matmul(
                                  bank[:, :], wo_p[:, dm * 128:(dm + 1) * 128],
                                  oT_h[:, blk * TQB:(blk + 1) * TQB],
                                  start=True, stop=True), inc="sPE")
                    p.wait("vector", "sPE", pv)
                    dst = aoh[:, dm * HALF + blk * TQB:
                              dm * HALF + blk * TQB + TQB]
                    if pair == 0:
                        wo_ev[i] = p.op("vector", lambda e, dst=dst, bank=bank:
                                        e.tensor_copy(dst, bank[:, :]), inc="sVE")
                    else:
                        wo_ev[i] = p.op("vector", lambda e, dst=dst, bank=bank:
                                        e.tensor_add(dst, dst, bank[:, :]),
                                        inc="sVE")
                p.barrier()
            if half == 0:
                tap("ao", aoh[:, 0:NC * HALF], (128, NC * HALF))
            xrl = tts
            for blk in range(min(NBH, int(os.environ.get('MK_P4NB', '9'))) if _p4 else 0):
                base = hbase + blk * TQB

                def aoc(c, blk=blk):
                    src_t = xnh if "p4xn" in _bs else aoh
                    return src_t[:, c * HALF + blk * TQB: c * HALF + blk * TQB + TQB]
                vv = ln_stats(aoc, TQB)
                p.wait("sync", "sSE")
                p.wait("sync", "sPE")
                ld_v = {0: 0, 1: 0}
                st_v = {0: 0, 1: 0}
                for c in range(0 if "p4ln" in _bs else NC):
                    par = c % 2
                    lsem = "sLD" if par == 0 else "sLDb"
                    ssem = "sST" if par == 0 else "sSTb"
                    buf = sqs[:, par * 512: par * 512 + TQB]
                    if ld_v[par]:
                        p.wait("sync", lsem, ld_v[par])
                    if st_v[par]:
                        p.wait("sync", ssem, st_v[par])
                    ld_v[par] = ldrow(buf, scx[c * 128:(c + 1) * 128,
                                               base:base + TQB], sem=lsem)
                    p.wait("vector", "sVE", vv)
                    p.op("vector", lambda e, c=c, aoc=aoc: e.tensor_mul(
                        xrl[:, 0:TQB], aoc(c), absb[:, 0:TQB]))
                    p.op("vector", lambda e: e.tensor_add(
                        xrl[:, 0:TQB], xrl[:, 0:TQB], absb[:, 512:1024]))
                    p.wait("vector", lsem, ld_v[par])
                    wv = p.op("vector", lambda e, buf=buf: e.tensor_add(
                        buf, buf, xrl[:, 0:TQB]), inc="sVE")
                    p.wait("gpsimd", "sVE", wv)
                    if st_v[par]:
                        p.wait("gpsimd", ssem, st_v[par])
                    st_v[par] = p.op("gpsimd", lambda e, c=c, base=base, buf=buf:
                                     e.dma_start(
                                         out=scx2[c * 128:(c + 1) * 128,
                                                  base:base + TQB],
                                         in_=buf), inc=ssem, n=16)
                p.barrier()
        A.release()
        A.release()
        tap("x2", sqs[:, 512:1024], (128, 512))

        w1q = A.alloc(BF16, 128, NC * HID)
        w2q = A.alloc(BF16, 128, 32 * D)
        x2c = A.alloc(F32, 128, 2 * NC * 130)
        wld = x2c[0:128, 0:1024]
        xq = A.alloc(BF16, 128, NC * 130)
        hbuf = A.alloc(F32, 128, 2 * 130)
        ysc = A.alloc(F32, 128, 130)
        s_sb = A.alloc(F32, 128, 32 * FT)
        sq_sb = A.alloc(BF16, 128, 32 * FT)
        ffnb = A.alloc(F32, 128, NC * FT)
        x3b = A.alloc(F32, 128, FT)
        assert A.hi <= 210432, f"arena overflow: {A.hi}"

        def quant_weights(wT, nchunks, ccols, qdst, statcol):
            nq = ccols // 512
            ld_v = {0: 0, 1: 0}
            use_v = {0: 0, 1: 0}
            def wpiece(idx, c, qq):
                par = idx % 2
                lsem = "sLD" if par == 0 else "sLDb"
                qeng = "sync" if par == 0 else "scalar"
                buf = wld[:, par * 512:(par + 1) * 512]
                if ld_v[par]:
                    p.wait(qeng, lsem, ld_v[par])
                if use_v[par]:
                    p.wait(qeng, "sVE", use_v[par])
                src_ap = wT[c * 128:(c + 1) * 128, qq * 512:(qq + 1) * 512]
                ld_v[par] = p.op(qeng, lambda e, d=buf, s=src_ap:
                                 e.dma_start(out=d, in_=s), inc=lsem, n=16)
                p.wait("vector", lsem, ld_v[par])
                return buf
            npieces = nchunks * nq
            assert npieces <= 64
            for idx, (c, qq) in enumerate(
                    (c, qq) for c in range(nchunks) for qq in range(nq)):
                buf = wpiece(idx, c, qq)
                use_v[idx % 2] = p.op("vector", lambda e, buf=buf, idx=idx:
                                      e.reduce_sum(
                    tts[:, 64 + idx:65 + idx], buf, axis=AX.X,
                    apply_absolute_value=True), inc="sVE")
            vspace()
            vspace()
            p.op("vector", lambda e, n=npieces: e.reduce_sum(
                wstat[:, 1:2], tts[:, 64:64 + n], axis=AX.X))
            vspace()
            vv = p.marker("vector", "sVE")
            p.wait("tensor", "sVE", vv)
            pv = p.op("tensor", lambda e: nc.tensor.matmul(
                pb[0][0:1, 0:1], wstat[:, 1:2], ones_col,
                start=True, stop=True), inc="sPE")
            p.wait("vector", "sPE", pv)
            p.op("vector", lambda e, n=nchunks * 128 * ccols: e.tensor_scalar(
                wrow[0:1, statcol:statcol + 1], pb[0][0:1, 0:1], 1.0 / n, 1e-5,
                ALU.mult, ALU.max))
            vspace()
            vv = p.marker("vector", "sVE")
            p.wait("tensor", "sVE", vv)
            pv = p.op("tensor", lambda e: nc.tensor.matmul(
                pb[1][:, 0:1], ones_row, wrow[0:1, statcol:statcol + 1],
                start=True, stop=True), inc="sPE")
            p.wait("vector", "sPE", pv)
            p.op("vector", lambda e: e.tensor_copy(wstat[:, 4:5], pb[1][:, 0:1]))
            vspace()
            vspace()
            p.op("vector", lambda e: e.reciprocal(wstat[:, 2:3], wstat[:, 4:5]))
            vspace()
            vspace()
            for idx, (c, qq) in enumerate(
                    (c, qq) for c in range(nchunks) for qq in range(nq)):
                buf = wpiece(idx, c, qq)
                p.op("vector", lambda e, buf=buf: e.tensor_scalar(
                    buf, buf, wstat[:, 2:3], None, ALU.mult))
                p.op("vector", lambda e, buf=buf: e.tensor_scalar(
                    buf, buf, R23, None, ALU.add))
                p.op("vector", lambda e, buf=buf: e.tensor_scalar(
                    buf, buf, -R23, None, ALU.add))
                use_v[idx % 2] = p.op("vector", lambda e, c=c, qq=qq, buf=buf:
                                      e.tensor_scalar(
                    qdst[:, c * ccols + qq * 512: c * ccols + (qq + 1) * 512],
                    buf, -1.0, 1.0, ALU.max, ALU.min), inc="sVE")
        if _p5:
            quant_weights(din["w1T"], NC, HID, w1q, 0)
            tap("wst", wstat[:, 0:4], (128, 4))
            tap("wrr", wrow[0:1, 0:8], (1, 8))
            p.barrier()
            quant_weights(din["w2T"], 32, D, w2q, 4)
            p.barrier()
        tap("w1q", w1q[:, 0:2048].bitcast(F32), (128, 1024))
        tap("w2q", w2q[:, 0:2048].bitcast(F32), (128, 1024))

        for fc in range(_nfc if _p5 else 0):
            t0 = fc * FT
            lo = max(t0 - 1, 0)
            hi = min(t0 + FT + 1, T)
            npos = hi - lo
            pofs = lo - (t0 - 1)
            xo = (fc % 2) * (NC * 130)
            if fc == 0:
                for c in range(NC):
                    ldrow(x2c[:, c * 130: c * 130 + npos],
                          scx2[c * 128:(c + 1) * 128, lo:lo + npos])
                p.barrier()
            if fc + 1 < _nfc:
                t0n = (fc + 1) * FT
                lon = max(t0n - 1, 0)
                nposn = min(t0n + FT + 1, T) - lon
                xon = ((fc + 1) % 2) * (NC * 130)
                for c in range(NC):
                    ldrow(x2c[:, xon + c * 130: xon + c * 130 + nposn],
                          scx2[c * 128:(c + 1) * 128, lon:lon + nposn])

            def x2ch(c, npos=npos, xo=xo):
                return x2c[:, xo + c * 130: xo + c * 130 + npos]
            if fc == 0:
                tap("x2c0", x2c[:, 0:NC * 130], (128, NC * 130))
            vv = ln_stats(x2ch, npos)

            def xn2c(c, npos=npos):
                return s_sb[:, c * 130: c * 130 + npos]
            p.wait("vector", "sVE", vv)
            for c in range(NC):
                p.op("vector", lambda e, c=c, npos=npos, x2ch=x2ch: e.tensor_mul(
                    tts[:, 0:npos], x2ch(c), absb[:, 0:npos]))
                p.op("vector", lambda e, c=c, npos=npos, xn2c=xn2c: e.tensor_add(
                    xn2c(c), tts[:, 0:npos], absb[:, 512:512 + npos]), inc="sVE")
            xv = s_sb[:, 0:NC * 130].rearrange(
                "p (c t) -> p t c", c=NC, t=130)[:, 0:npos, :]
            p.op("vector", lambda e, xv=xv, npos=npos: e.reduce_max(
                sqs[:, 0:npos], xv, axis=AX.X, apply_absolute_value=True))
            vv = p.marker("vector", "sVE")
            p.wait("tensor", "sVE", vv)
            ncols = [128, npos - 128] if npos > 128 else [npos]
            rv = 0
            for i, w in enumerate(ncols):
                if i:
                    p.wait("tensor", "sVE")
                pv = p.op("tensor", lambda e, i=i, w=w: nc.tensor.transpose(
                    pb[4][0:w, 0:128], sqs[:, i * 128: i * 128 + w], ident),
                    inc="sPE")
                p.wait("vector", "sPE", pv)
                rv = p.op("vector", lambda e, i=i, w=w: e.reduce_max(
                    tts[0:w, i: i + 1], pb[4][0:w, 0:128], axis=AX.X), inc="sVE")
            p.wait("tensor", "sVE", rv)
            for i, w in enumerate(ncols):
                pv = p.op("tensor", lambda e, i=i, w=w: nc.tensor.transpose(
                    pb[5][0:1, i * 128: i * 128 + w], tts[0:w, i:i + 1],
                    ident[0:w, 0:w]), inc="sPE")
            p.wait("vector", "sPE", pv)
            m1c = rowB[0:1, 1024:1024 + npos]
            s1r = rowA[0:1, 3072:3072 + npos]
            g1r = rowA[0:1, 3328:3328 + npos]
            vspace()
            p.op("vector", lambda e, npos=npos, m1c=m1c: e.tensor_scalar(
                m1c, pb[5][0:1, 0:npos], 1e-5, None, ALU.max))
            vspace()
            p.op("vector", lambda e, s1r=s1r, m1c=m1c: e.reciprocal(s1r, m1c))
            vspace()
            p.op("vector", lambda e, s1r=s1r: e.tensor_scalar(
                s1r, s1r, 127.0, None, ALU.mult))
            vspace()
            vv = p.op("vector", lambda e, g1r=g1r, m1c=m1c: e.tensor_scalar(
                g1r, m1c, 1.0 / 127.0, wrow[0:1, 0:1], ALU.mult, ALU.mult),
                inc="sVE")
            p.wait("tensor", "sVE", vv)
            p.op("tensor", lambda e, npos=npos, s1r=s1r: nc.tensor.matmul(
                pb[2][:, 0:npos], ones_row, s1r, start=True, stop=True), inc="sPE")
            pv = p.op("tensor", lambda e, npos=npos, g1r=g1r: nc.tensor.matmul(
                pb[3][:, 0:npos], ones_row, g1r, start=True, stop=True), inc="sPE")
            p.wait("vector", "sPE", pv)
            s1bc = absb[:, 0:npos]
            g1bc = absb[:, 512:512 + npos]
            vspace()
            p.op("vector", lambda e, npos=npos, s1bc=s1bc: e.tensor_copy(
                s1bc, pb[2][:, 0:npos]))
            vspace()
            p.op("vector", lambda e, npos=npos, g1bc=g1bc: e.tensor_copy(
                g1bc, pb[3][:, 0:npos]), inc="sVE")
            for c in range(NC):
                p.op("vector", lambda e, c=c, npos=npos, xn2c=xn2c, s1bc=s1bc:
                     e.tensor_mul(tts[:, 0:npos], xn2c(c), s1bc))
                p.op("vector", lambda e, npos=npos: e.tensor_scalar(
                    tts[:, 0:npos], tts[:, 0:npos], R23, None, ALU.add))
                p.op("vector", lambda e, c=c, npos=npos: e.tensor_scalar(
                    xq[:, c * 130: c * 130 + npos], tts[:, 0:npos],
                    -R23, None, ALU.add), inc="sVE")
            p.wait("tensor", "sVE")
            if fc == 0:
                tap("xq0", xq[:, 0:NC * 130].bitcast(F32), (128, NC * 65))
            mul_v = [0] * 32
            id_v = [0] * 32
            for hc in range(32):
                bank = pb[hc % 4]
                if hc >= 4:
                    p.wait("tensor", "sVE", mul_v[hc - 4])
                for c in range(NC):
                    p.op("tensor", lambda e, hc=hc, c=c, npos=npos, bank=bank:
                         nc.tensor.matmul(
                             bank[:, 0:npos],
                             w1q[:, c * HID + hc * 128: c * HID + (hc + 1) * 128],
                             xq[:, c * 130: c * 130 + npos],
                             start=(c == 0), stop=(c == NC - 1)), inc="sPE")
                p.wait("vector", "sPE")
                hb = hbuf[:, (hc % 2) * 130:(hc % 2) * 130 + 130]
                if hc >= 2:
                    p.wait("vector", "sSE", id_v[hc - 2])
                if pofs == 1:
                    p.op("vector", lambda e, hb=hb: e.memset(hb[:, 0:1], 0.0))
                if hi == T:
                    p.op("vector", lambda e, hb=hb: e.memset(hb[:, 129:130], 0.0))
                mul_v[hc] = p.op(
                    "vector",
                    lambda e, hb=hb, pofs=pofs, npos=npos, bank=bank, g1bc=g1bc:
                    e.tensor_mul(hb[:, pofs:pofs + npos], bank[:, 0:npos],
                                 g1bc), inc="sVE")
                if fc == 0 and hc == 0:
                    tap("h0", hb[:, 0:130], (128, 130))
                p.wait("scalar", "sVE", mul_v[hc])
                id_v[hc] = p.op("scalar", lambda e, hc=hc, hb=hb: e.activation(
                    ysc[:, 0:FT], hb[:, 1:1 + FT], AF.Identity,
                    bias=convb_sb[:, hc:hc + 1],
                    scale=convw_sb[:, hc * 3 + 1: hc * 3 + 2]), inc="sSE")
                p.wait("vector", "sSE", id_v[hc])
                p.op("vector", lambda e, hc=hc, hb=hb: e.scalar_tensor_tensor(
                    ysc[:, 0:FT], hb[:, 0:FT], convw_sb[:, hc * 3: hc * 3 + 1],
                    ysc[:, 0:FT], ALU.mult, ALU.add))
                vv = p.op("vector", lambda e, hc=hc, hb=hb: e.scalar_tensor_tensor(
                    ysc[:, 0:FT], hb[:, 2:2 + FT],
                    convw_sb[:, hc * 3 + 2: hc * 3 + 3],
                    ysc[:, 0:FT], ALU.mult, ALU.add), inc="sVE")
                p.wait("scalar", "sVE", vv)
                p.op("scalar", lambda e, hc=hc: e.activation(
                    s_sb[:, hc * FT:(hc + 1) * FT], ysc[:, 0:FT], AF.Silu),
                    inc="sSE")
            p.wait("vector", "sSE")
            if fc == 0:
                tap("s0", s_sb[:, 0:32 * FT], (128, 32 * FT))
            sv_ = s_sb[:, 0:32 * FT].rearrange("p (h t) -> p t h", h=32, t=FT)
            p.op("vector", lambda e, sv_=sv_: e.reduce_max(
                sqs[:, 0:FT], sv_, axis=AX.X, apply_absolute_value=True))
            vv = p.marker("vector", "sVE")
            p.wait("tensor", "sVE", vv)
            pv = p.op("tensor", lambda e: nc.tensor.transpose(
                pb[4][0:FT, 0:128], sqs[:, 0:FT], ident), inc="sPE")
            p.wait("vector", "sPE", pv)
            vv = p.op("vector", lambda e: e.reduce_max(
                tts[0:FT, 0:1], pb[4][0:FT, 0:128], axis=AX.X), inc="sVE")
            p.wait("tensor", "sVE", vv)
            pv = p.op("tensor", lambda e: nc.tensor.transpose(
                pb[5][0:1, 0:FT], tts[0:FT, 0:1], ident[0:FT, 0:FT]), inc="sPE")
            p.wait("vector", "sPE", pv)
            m2c = rowB[0:1, 2560:2560 + FT]
            s2r = rowA[0:1, 3072:3072 + FT]
            g2r = rowA[0:1, 3328:3328 + FT]
            vspace()
            p.op("vector", lambda e: e.tensor_scalar(
                m2c, pb[5][0:1, 0:FT], 1e-5, None, ALU.max))
            vspace()
            p.op("vector", lambda e: e.reciprocal(s2r, m2c))
            vspace()
            p.op("vector", lambda e: e.tensor_scalar(
                s2r, s2r, 127.0, None, ALU.mult))
            vspace()
            vv = p.op("vector", lambda e: e.tensor_scalar(
                g2r, m2c, 1.0 / 127.0, wrow[0:1, 4:5], ALU.mult, ALU.mult),
                inc="sVE")
            p.wait("tensor", "sVE", vv)
            p.op("tensor", lambda e: nc.tensor.matmul(
                pb[2][:, 0:FT], ones_row, s2r, start=True, stop=True), inc="sPE")
            pv = p.op("tensor", lambda e: nc.tensor.matmul(
                pb[3][:, 0:FT], ones_row, g2r, start=True, stop=True), inc="sPE")
            p.wait("vector", "sPE", pv)
            s2bc = absb[:, 0:FT]
            g2bc = absb[:, 512:512 + FT]
            vspace()
            p.op("vector", lambda e: e.tensor_copy(s2bc, pb[2][:, 0:FT]))
            vspace()
            p.op("vector", lambda e: e.tensor_copy(g2bc, pb[3][:, 0:FT]),
                 inc="sVE")
            sqv = [0] * 32
            for hc in range(32):
                p.op("vector", lambda e, hc=hc: e.tensor_mul(
                    tts[:, 0:FT], s_sb[:, hc * FT:(hc + 1) * FT], s2bc))
                p.op("vector", lambda e: e.tensor_scalar(
                    tts[:, 0:FT], tts[:, 0:FT], R23, None, ALU.add))
                sqv[hc] = p.op("vector", lambda e, hc=hc: e.tensor_scalar(
                    sq_sb[:, hc * FT:(hc + 1) * FT], tts[:, 0:FT],
                    -R23, None, ALU.add), inc="sVE")
            if fc == 0:
                p.wait("tensor", "sVE")
                tap("sq0", sq_sb[:, 0:4096].bitcast(F32), (128, 2048))
                tap("m2row", rowB[0:1, 2560:2560 + FT], (1, FT))
                tap("g2row", rowA[0:1, 3328:3328 + FT], (1, FT))
            for hc in range(32):
                p.wait("tensor", "sVE", sqv[hc])
                for dm in range(NC):
                    p.op("tensor", lambda e, dm=dm, hc=hc:
                         nc.tensor.matmul(
                             pb[dm][:, 0:FT],
                             w2q[:, hc * D + dm * 128: hc * D + (dm + 1) * 128],
                             sq_sb[:, hc * FT:(hc + 1) * FT],
                             start=(hc == 0), stop=(hc == 31)), inc="sPE")
            p.wait("vector", "sPE")
            for dm in range(NC):
                p.op("vector", lambda e, dm=dm, g2bc=g2bc: e.tensor_mul(
                    ffnb[:, dm * FT:(dm + 1) * FT], pb[dm][:, 0:FT], g2bc),
                    inc="sVE")

            if fc == 0:
                tap("ffnb0", ffnb[:, 0:NC * FT], (128, NC * FT))
            if fc == 1:
                tap("ffnb1", ffnb[:, 0:NC * FT], (128, NC * FT))

            def fch(c):
                return ffnb[:, c * FT:(c + 1) * FT]
            p.wait("scalar", "sVE")
            p.wait("tensor", "sVE")
            vv = ln_stats(fch, FT)
            p.wait("vector", "sVE", vv)
            yst_v = {0: 0, 1: 0}
            for c in range(NC):
                par = c % 2
                ssem = "sST" if par == 0 else "sSTb"
                buf = x3b[:, 0:FT] if par == 0 else ysc[:, 0:FT]
                p.op("vector", lambda e, c=c: e.tensor_mul(
                    tts[:, 0:FT], fch(c), absb[:, 0:FT]))
                p.op("vector", lambda e: e.tensor_add(
                    tts[:, 0:FT], tts[:, 0:FT], absb[:, 512:512 + FT]))
                if yst_v[par]:
                    p.wait("vector", ssem, yst_v[par])
                x2o = t0 - lo
                wv = p.op("vector", lambda e, c=c, x2o=x2o, buf=buf, xo=xo:
                          e.tensor_add(
                    buf, x2c[:, xo + c * 130 + x2o: xo + c * 130 + x2o + FT],
                    tts[:, 0:FT]), inc="sVE")
                p.wait("gpsimd", "sVE", wv)
                if yst_v[par]:
                    p.wait("gpsimd", ssem, yst_v[par])
                yst_v[par] = p.op("gpsimd", lambda e, c=c, t0=t0, buf=buf:
                                  e.dma_start(
                                      out=y[c * 128:(c + 1) * 128, t0:t0 + FT],
                                      in_=buf), inc=ssem, n=16)
            p.barrier()

        p.wait("tensor", "sVE")
        p.op("tensor", lambda e: nc.tensor.matmul(
            pbT[0][0:1, 0:128], ones_col, ident,
            start=True, stop=True), inc="sPE")
        p.wait("gpsimd", "sPE")
        if "delay" in _bs:
            for i in range(int(os.environ.get("MK_DELAY", "2000"))):
                p.op("vector", lambda e: e.tensor_copy(rowB[0:1, 2752:2760],
                                                       rowB[0:1, 2744:2752]))
            if "nodelaywait" not in _bs:
                p.marker("vector", "sVE")
                for eng in ENGINES:
                    p.wait(eng, "sVE")
        p.wait("gpsimd", "sST")
        p.wait("gpsimd", "sLD")

        with nc.Block() as block:
            @block.sync
            def _(e):
                for f in p.streams["sync"]:
                    f(e)

            @block.tensor
            def _(e):
                for f in p.streams["tensor"]:
                    f(e)

            @block.vector
            def _(e):
                for f in p.streams["vector"]:
                    f(e)

            @block.scalar
            def _(e):
                for f in p.streams["scalar"]:
                    f(e)

            @block.gpsimd
            def _(e):
                for f in p.streams["gpsimd"]:
                    f(e)

    return nc, dbg_out


def make_inputs(inputs):
    f32 = np.float32
    C = np.ascontiguousarray
    x = np.asarray(inputs["L0_attn_x"], f32)
    shared = {
        "wqT": C(np.asarray(inputs["L0_attn_wq"], f32).T),
        "wkaT": C(np.asarray(inputs["L0_attn_wk_a"], f32).T),
        "wkbT": C(np.asarray(inputs["L0_attn_wk_b"], f32).T),
        "wvaT": C(np.asarray(inputs["L0_attn_wv_a"], f32).T),
        "wvbT": C(np.asarray(inputs["L0_attn_wv_b"], f32).T),
        "woT": C(np.asarray(inputs["L0_attn_wo"], f32).T),
        "w1T": C(np.asarray(inputs["L0_ffn_w1"], f32).T),
        "w2T": C(np.asarray(inputs["L0_ffn_w2"], f32).T),
        "convw": C(np.asarray(inputs["L0_ffn_conv_w"], f32).reshape(HID, 3)),
        "convb": C(np.asarray(inputs["L0_ffn_conv_b"], f32).reshape(HID, 1)),
    }
    inv = (f32(1.0) / np.power(f32(10000.0),
                               np.arange(0, HD, 2, dtype=f32) / f32(HD))).astype(f32)
    ang = (np.arange(T, dtype=f32)[:, None] * inv[None, :]).astype(f32)
    cosf = np.cos(ang).astype(f32)[:NH, :8]
    sinf = np.sin(ang).astype(f32)[:NH, :8]
    ropeC = np.zeros((8, 8, 2, 8), f32)
    ropeS = np.zeros((8, 8, 2, 8), f32)
    for pr in range(8):
        for c in range(8):
            for j in range(2):
                ropeC[pr, c, j, :] = cosf[2 * pr + j]
                ropeS[pr, c, j, :] = sinf[2 * pr + j]
    e2 = np.zeros((128, 2), f32)
    e2[0:64, 0] = 1.0
    e2[64:128, 1] = 1.0
    sh = np.zeros((64, 128), f32)
    sh[np.arange(64), 64 + np.arange(64)] = 1.0
    shared.update({
        "ropeC": np.broadcast_to(ropeC.reshape(1, 1024), (128, 1024)).copy(),
        "ropeS": np.broadcast_to(ropeS.reshape(1, 1024), (128, 1024)).copy(),
        "ones_col": np.ones((128, 1), f32), "ones_row": np.ones((1, 128), f32),
        "e2": e2, "shift64": sh, "ident": np.eye(128, dtype=f32),
    })
    in_maps = []
    for core in range(8):
        m = dict(shared)
        m["xT"] = C(x[core].T)
        in_maps.append(m)
    return in_maps


def kernel(**inputs):
    in_maps = make_inputs(inputs)
    nc, _ = build_program()
    res = run_bass_kernel_spmd(nc, in_maps, core_ids=list(range(8)))
    out = np.stack([np.ascontiguousarray(r["y"].T) for r in res.results])
    return out.astype(np.float32)


# version 09
# speedup vs baseline: 1.4074x; 1.0144x over previous
import os
import numpy as np
from contextlib import ExitStack
import concourse.bass as bass
import concourse.mybir as mybir
from concourse.bass_utils import run_bass_kernel_spmd

F32 = mybir.dt.float32
BF16 = mybir.dt.bfloat16
U8 = mybir.dt.uint8
AF = mybir.ActivationFunctionType
ALU = mybir.AluOpType
AX = mybir.AxisListType

D = 1024
T = 2048
NH = 16
HD = 64
LAT = 256
HID = 4096
NC = 8
TQB = 512
NB = T // TQB
HALF = T // 2
NBH = HALF // TQB
NTK = T // 128
FT = 128
NFC = T // FT
R23 = float(1.5 * 2.0 ** 23)

ENGINES = ("sync", "tensor", "vector", "scalar", "gpsimd")
_DTS = {F32: 4, BF16: 2, U8: 1}


class Arena:

    def __init__(self, t):
        self.t = t
        self.cur = 0
        self.marks = []
        self.hi = 0

    def alloc(self, dtype, parts, cols):
        off = (self.cur + 15) & ~15
        nbytes = cols * _DTS[dtype]
        self.cur = off + nbytes
        self.hi = max(self.hi, self.cur)
        return self.t[0:parts, off:off + nbytes].bitcast(dtype)

    def mark(self):
        self.marks.append(self.cur)

    def release(self):
        self.cur = self.marks.pop()


class Prog:

    def __init__(self, nc):
        self.nc = nc
        self.streams = {e: [] for e in ENGINES}
        self.sems = {}
        self.mkr_v = None
        self.mkr_s = None
        self.remap_gpsimd = True

    def new_sem(self, stack, name):
        h = stack.enter_context(self.nc.semaphore(name))
        self.sems[name] = [h, 0]

    def val(self, name):
        return self.sems[name][1]

    def bump(self, name, n=1):
        self.sems[name][1] += n
        return self.sems[name][1]

    def em(self, eng, fn):
        self.streams[eng].append(fn)

    def _q(self, eng):
        return "sync" if eng == "gpsimd" and self.remap_gpsimd else eng

    def wait(self, eng, name, v=None):
        h, cur = self.sems[name]
        if v is None:
            v = cur
        assert v <= cur, f"wait on future value of {name}: {v} > {cur}"
        if v > 0:
            self.em(self._q(eng), lambda e, h=h, v=v: e.wait_ge(h, v))

    def op(self, eng, make, inc=None, n=1):
        eng = self._q(eng)
        if inc is None:
            self.em(eng, lambda e, make=make: make(e))
            return None
        h = self.sems[inc][0]
        v = self.bump(inc, n)
        self.em(eng, lambda e, make=make, h=h, n=n: make(e).then_inc(h, n))
        return v

    def marker(self, eng, sem):
        h = self.sems[sem][0]
        v = self.bump(sem, 1)
        if eng == "vector" and self.mkr_v is not None:
            d, s = self.mkr_v
            self.em(eng, lambda e, h=h, d=d, s=s:
                    e.tensor_copy(d, s).then_inc(h, 1))
        elif eng == "scalar" and self.mkr_s is not None:
            d, s = self.mkr_s
            self.em(eng, lambda e, h=h, d=d, s=s:
                    e.memzero(d).then_inc(h, 1))
        else:
            self.em(eng, lambda e, h=h: e.nop().then_inc(h, 1))
        return v

    def barrier(self):
        for s in ("sLD", "sLDb", "sST", "sSTb"):
            self.wait("sync", s)
        for eng in ENGINES:
            self.marker(eng, "sBAR")
        tgt = self.val("sBAR")
        h = self.sems["sBAR"][0]
        for eng in ENGINES:
            self.em(eng, lambda e, h=h, v=tgt: e.wait_ge(h, v))


def build_program(dbg=()):
    _bs = os.environ.get("MK_STAGE", "")
    _npair = int(os.environ.get("MK_NPAIR", "8"))
    _nhalf = int(os.environ.get("MK_NHALF", "2"))
    _nfc = int(os.environ.get("MK_NFC", str(NFC)))
    _p3 = "nop3" not in _bs
    _p4 = "nop4" not in _bs
    _p5 = "nop5" not in _bs
    nc = bass.Bass("TRN2", target_bir_lowering=False, debug=False,
                   detect_race_conditions=False)
    din = {}
    for name, shape in [
        ("xT", [D, T]), ("wqT", [D, D]), ("wkaT", [D, LAT]), ("wkbT", [LAT, D]),
        ("wvaT", [D, LAT]), ("wvbT", [LAT, D]), ("woT", [D, D]),
        ("w1T", [D, HID]), ("w2T", [HID, D]),
        ("convw", [HID, 3]), ("convb", [HID, 1]),
        ("ropeC", [128, 1024]), ("ropeS", [128, 1024]),
        ("ones_col", [128, 1]), ("ones_row", [1, 128]), ("e2", [128, 2]),
        ("shift64", [64, 128]), ("ident", [128, 128]),
    ]:
        din[name] = nc.dram_tensor(name, shape, F32, kind="ExternalInput").ap()
    y = nc.dram_tensor("y", [D, T], F32, kind="ExternalOutput").ap()
    scx = nc.dram_tensor("scx", [D, T], F32, kind="Internal").ap()
    scxn = nc.dram_tensor("scxn", [D, T], F32, kind="Internal").ap()
    scx2 = nc.dram_tensor("scx2", [D, T], F32, kind="Internal").ap()
    dbg_out = {}

    p = Prog(nc)
    with ExitStack() as S:
        for s in ("sLD", "sLDb", "sST", "sSTb", "sPE", "sVE", "sSE", "sGP", "sSY", "sBAR"):
            p.new_sem(S, s)
        pbT = [S.enter_context(nc.psum_tensor(f"pbT{i}", [128, 1024], F32))
               for i in range(4)]
        pb = [pbT[i // 2][:, (i % 2) * 512:(i % 2 + 1) * 512] for i in range(8)]
        A = Arena(S.enter_context(nc.sbuf_tensor("arena", [128, 210432], U8)))

        ident = A.alloc(F32, 128, 128)
        ones_col = A.alloc(F32, 128, 1)
        ones_row = A.alloc(F32, 1, 128)
        e2sb = A.alloc(F32, 128, 2)
        shift64 = A.alloc(F32, 64, 128)
        nrm = A.alloc(F32, 2, 16)
        negb = A.alloc(F32, 128, 2)
        wrow = A.alloc(F32, 1, 8)
        wstat = A.alloc(F32, 128, 8)
        convw_sb = A.alloc(F32, 128, 96)
        convb_sb = A.alloc(F32, 128, 32)
        rowA = A.alloc(F32, 1, 3600)
        rowB = A.alloc(F32, 1, 2816)
        absb = A.alloc(F32, 128, 2 * 512)
        sqs = A.alloc(F32, 128, 2 * 512)
        tts = A.alloc(F32, 128, 512)

        p.mkr_v = (rowB[0:1, 2752:2760], rowB[0:1, 2744:2752])
        p.mkr_s = (rowB[0:1, 2736:2744], rowB[0:1, 2728:2736])
        p.op("vector", lambda e: e.memset(rowB[0:1, 2744:2760], 0.0))
        p.op("scalar", lambda e: e.memzero(rowB[0:1, 2728:2744]))

        def vspace():
            p.op("vector", lambda e: e.tensor_copy(rowB[0:1, 2784:2800],
                                                   rowB[0:1, 2768:2784]))

        def ldrow(dst_ap, src_ap, sem="sLD"):
            return p.op("sync", lambda e, d=dst_ap, s=src_ap:
                        e.dma_start(out=d, in_=s), inc=sem, n=16)

        def tap(name, ap, shape):
            if name not in dbg:
                return
            t = nc.dram_tensor("dbg_" + name, list(shape), F32,
                               kind="ExternalOutput").ap()
            dbg_out[name] = t
            p.barrier()
            p.op("gpsimd", lambda e, t=t, ap=ap: e.dma_start(out=t, in_=ap),
                 inc="sST", n=16)
            p.barrier()

        ldrow(ident, din["ident"])
        ldrow(ones_col, din["ones_col"])
        ldrow(ones_row, din["ones_row"])
        ldrow(e2sb, din["e2"])
        ldrow(shift64, din["shift64"])
        for hc in range(32):
            ldrow(convw_sb[:, hc * 3:(hc + 1) * 3],
                  din["convw"][hc * 128:(hc + 1) * 128, :])
            ldrow(convb_sb[:, hc:hc + 1], din["convb"][hc * 128:(hc + 1) * 128, :])

        def ln_stats(get_chunk, npos):
            msq_v = [0] * NC
            for c in range(NC):
                if c >= 2:
                    p.wait("scalar", "sPE", msq_v[c - 2])
                sv = p.op("scalar", lambda e, c=c, npos=npos: e.activation(
                    sqs[:, (c % 2) * 512:(c % 2) * 512 + npos], get_chunk(c),
                    AF.Square), inc="sSE")
                p.op("tensor", lambda e, c=c, npos=npos: nc.tensor.matmul(
                    pbT[0][0:1, 0:npos], ones_col, get_chunk(c),
                    start=(c == 0), stop=(c == NC - 1)), inc="sPE")
                p.wait("tensor", "sSE", sv)
                msq_v[c] = p.op("tensor", lambda e, c=c, npos=npos: nc.tensor.matmul(
                    pbT[0][0:1, 512:512 + npos], ones_col,
                    sqs[:, (c % 2) * 512:(c % 2) * 512 + npos],
                    start=(c == 0), stop=(c == NC - 1)), inc="sPE")
            p.wait("vector", "sPE")
            p.wait("scalar", "sPE")
            cut = float(os.environ.get("MK_LNCUT", "99"))
            if cut < 2:
                return p.marker("vector", "sVE")
            mu = rowA[0:1, 0:npos]
            e2x = rowA[0:1, 512:512 + npos]
            var = rowA[0:1, 1024:1024 + npos]
            ve = rowA[0:1, 1536:1536 + npos]
            rr = rowA[0:1, 2048:2048 + npos]
            bb = rowA[0:1, 2560:2560 + npos]
            t0 = rowB[0:1, 0:npos]
            raw = rowB[0:1, 1280:2304]
            p.op("vector", lambda e: e.tensor_copy(raw, pbT[0][0:1, 0:1024]))
            if cut < 3:
                return p.marker("vector", "sVE")
            p.op("vector", lambda e, npos=npos: e.tensor_scalar(
                mu, raw[0:1, 0:npos], 1.0 / D, None, ALU.mult))
            p.op("vector", lambda e, npos=npos: e.tensor_scalar(
                e2x, raw[0:1, 512:512 + npos], 1.0 / D, None, ALU.mult))
            p.op("vector", lambda e: e.tensor_mul(var, mu, mu))
            vspace()
            p.op("vector", lambda e: e.tensor_sub(var, e2x, var))
            vspace()
            vv = p.op("vector", lambda e: e.tensor_scalar(
                ve, var, 1e-5, None, ALU.add), inc="sVE")
            if cut < 4:
                return vv
            p.wait("scalar", "sVE", vv)
            sv = p.op("scalar", lambda e: e.activation(rr, ve, AF.Sqrt), inc="sSE")
            p.wait("vector", "sSE", sv)
            p.op("vector", lambda e: e.reciprocal(rr, rr))
            vspace()
            p.op("vector", lambda e: e.tensor_mul(t0, rr, rr))
            vspace()
            p.op("vector", lambda e: e.tensor_mul(t0, t0, ve))
            vspace()
            p.op("vector", lambda e: e.tensor_scalar(t0, t0, -0.5, 1.5,
                                                     ALU.mult, ALU.add))
            vspace()
            p.op("vector", lambda e: e.tensor_mul(rr, rr, t0))
            vspace()
            vv = p.op("vector", lambda e: e.scalar_tensor_tensor(
                bb, mu, -1.0, rr, ALU.mult, ALU.mult), inc="sVE")
            if cut < 5:
                return vv
            p.wait("tensor", "sVE", vv)
            p.op("tensor", lambda e, npos=npos: nc.tensor.matmul(
                pbT[1][:, 0:npos], ones_row, rr, start=True, stop=True), inc="sPE")
            if cut < 45.5:
                return p.marker("vector", "sVE")
            pv = p.op("tensor", lambda e, npos=npos: nc.tensor.matmul(
                pbT[1][:, 512:512 + npos], ones_row, bb,
                start=True, stop=True), inc="sPE")
            if cut < 6:
                return p.marker("vector", "sVE")
            p.wait("vector", "sPE", pv)
            return p.op("vector", lambda e: e.tensor_copy(
                absb[:, 0:1024], pbT[1][:, 0:1024]), inc="sVE")

        A.mark()
        kcsb = A.alloc(F32, 128, 2 * T)
        kvsb = A.alloc(F32, 128, 2 * T)
        wvbT_sb = A.alloc(F32, 128, 2 * D)
        ropec = A.alloc(F32, 128, 1024)
        ropes = A.alloc(F32, 128, 1024)
        A.mark()
        xblk = A.alloc(F32, 128, NC * TQB)
        wka_sb = A.alloc(F32, 128, NC * LAT)
        wva_sb = A.alloc(F32, 128, NC * LAT)
        for c in range(NC):
            ldrow(wka_sb[:, c * LAT:(c + 1) * LAT],
                  din["wkaT"][c * 128:(c + 1) * 128, :])
            ldrow(wva_sb[:, c * LAT:(c + 1) * LAT],
                  din["wvaT"][c * 128:(c + 1) * 128, :])
        p.barrier()
        for blk in range(NB):
            base = blk * TQB

            def xc(c):
                return xblk[:, c * TQB:(c + 1) * TQB]
            lds = [ldrow(xc(c), din["xT"][c * 128:(c + 1) * 128, base:base + TQB])
                   for c in range(NC)]
            if blk == 0:
                tap("x0", xblk[:, 0:NC * TQB], (128, NC * TQB))
            p.wait("gpsimd", "sLD", lds[-1])
            for c in range(NC):
                p.op("gpsimd", lambda e, c=c, base=base: e.dma_start(
                    out=scx[c * 128:(c + 1) * 128, base:base + TQB], in_=xc(c)),
                    inc="sST", n=16)
            xspill = p.val("sST")
            p.wait("gpsimd", "sST", xspill)
            p.wait("tensor", "sLD", lds[-1])
            p.wait("scalar", "sLD", lds[-1])
            vv = ln_stats(xc, TQB)
            if blk == 0:
                tap("ab0", absb[:, 0:1024], (128, 1024))
            p.wait("vector", "sVE", vv)
            p.wait("vector", "sST", xspill)
            for c in range(NC):
                p.op("vector", lambda e, c=c: e.tensor_mul(
                    tts[:, 0:TQB], xc(c), absb[:, 0:TQB]))
                p.op("vector", lambda e, c=c: e.tensor_add(
                    xc(c), tts[:, 0:TQB], absb[:, 512:1024]), inc="sVE")
            xn_done = p.val("sVE")
            if blk == 0:
                tap("xn0", xblk[:, 0:NC * TQB], (128, NC * TQB))
                if "xncpy" in dbg:
                    for c in range(NC):
                        p.op("vector", lambda e, c=c: e.tensor_copy(
                            kvsb[:, c * TQB:(c + 1) * TQB], xc(c)), inc="sVE")
                    tap("xncpy", kvsb[:, 0:NC * TQB], (128, NC * TQB))
            for c in range(NC):
                p.wait("gpsimd", "sVE", xn_done)
                p.op("gpsimd", lambda e, c=c, base=base: e.dma_start(
                    out=scxn[c * 128:(c + 1) * 128, base:base + TQB], in_=xc(c)),
                    inc="sST", n=16)
            p.wait("tensor", "sVE", xn_done)
            for ti, (wsb, dst) in enumerate(((wka_sb, kcsb), (wva_sb, kvsb))):
                for lc in range(2):
                    bank = pb[(ti * 2 + lc) % 4 + 4]
                    for c in range(NC):
                        p.op("tensor", lambda e, wsb=wsb, lc=lc, c=c, bank=bank:
                             nc.tensor.matmul(
                                 bank[:, :],
                                 wsb[:, c * LAT + lc * 128: c * LAT + (lc + 1) * 128],
                                 xc(c), start=(c == 0), stop=(c == NC - 1)),
                             inc="sPE")
                    p.wait("vector", "sPE")
                    p.op("vector", lambda e, dst=dst, lc=lc, base=base, bank=bank:
                         e.tensor_copy(
                             dst[:, lc * T + base: lc * T + base + TQB], bank[:, :]),
                         inc="sVE")
            p.wait("tensor", "sVE")
            p.barrier()
        tap("kc", kcsb[:, 0:2 * T], (128, 2 * T))
        tap("kv", kvsb[:, 0:2 * T], (128, 2 * T))
        A.release()

        if "nowl" not in _bs:
            for lc in range(2):
                ldrow(wvbT_sb[:, lc * D:(lc + 1) * D],
                      din["wvbT"][lc * 128:(lc + 1) * 128, :])
            ldrow(ropec, din["ropeC"])
            ldrow(ropes, din["ropeS"])
        A.mark()
        xnh = A.alloc(F32, 128, NC * HALF)
        aoh = A.alloc(F32, 128, NC * HALF)
        wq_sl = A.alloc(F32, 128, 2 * NC * 128)
        wk_sl = A.alloc(F32, 128, 2 * 2 * 128)
        wo_sl = A.alloc(F32, 128, 2 * D)
        q_h = A.alloc(F32, 128, HALF)
        k_pair = A.alloc(F32, 128, T)
        v_pad = A.alloc(F32, 128, 2 * 16 * 65)
        attnT = A.alloc(F32, 128, 2 * TQB)
        oT_h = A.alloc(F32, 128, HALF)
        tmpB = A.alloc(F32, 64, TQB)
        rs_bc = A.alloc(F32, 64, TQB)

        def rope_cols(wsl, nchunks, pair):
            for c in range(nchunks):
                for j in range(2):
                    o = c * 128 + 64 * j
                    E = wsl[:, o:o + 16:2]
                    O = wsl[:, o + 1:o + 17:2]
                    rc_ = ropec[:, pair * 128 + j * 8: pair * 128 + (j + 1) * 8]
                    rs_ = ropes[:, pair * 128 + j * 8: pair * 128 + (j + 1) * 8]
                    t1, t2 = tts[:, 0:8], tts[:, 8:16]
                    t3, t4 = tts[:, 16:24], tts[:, 24:32]
                    p.op("vector", lambda e, E=E, rc_=rc_, t1=t1: e.tensor_mul(t1, E, rc_))
                    p.op("vector", lambda e, O=O, rs_=rs_, t2=t2: e.tensor_mul(t2, O, rs_))
                    p.op("vector", lambda e, E=E, rs_=rs_, t3=t3: e.tensor_mul(t3, E, rs_))
                    p.op("vector", lambda e, O=O, rc_=rc_, t4=t4: e.tensor_mul(t4, O, rc_))
                    p.op("vector", lambda e, E=E, t1=t1, t2=t2: e.tensor_sub(E, t1, t2))
                    p.op("vector", lambda e, O=O, t3=t3, t4=t4: e.tensor_add(O, t3, t4),
                         inc="sVE")

        for _i in range(int(os.environ.get("MK_DUMDMA", "0"))):
            ldrow(ident, din["ident"])
        if "dumb" in _bs:
            p.barrier()
            p.barrier()
        if "dumw" in _bs:
            for c in range(NC):
                ldrow(wq_sl[:, c * 128:(c + 1) * 128],
                      din["wqT"][c * 128:(c + 1) * 128, 0:128])
            for lc in range(2):
                ldrow(wk_sl[:, lc * 128:(lc + 1) * 128],
                      din["wkbT"][lc * 128:(lc + 1) * 128, 0:128])
            ldrow(wo_sl[:, 0:D], din["woT"][0:128, :])
        if "nomemset" not in _bs:
            p.op("vector", lambda e: e.memset(nrm[:, :], 0.0))
        vm = p.op("vector", lambda e: e.memset(
            v_pad[:, 64:65] if "nomemset" in _bs else v_pad[:, 64:32 * 65:65],
            1.0), inc="sVE")

        def wviews(par):
            return (wq_sl[:, par * NC * 128:(par + 1) * NC * 128],
                    wk_sl[:, par * 256:(par + 1) * 256],
                    wo_sl[:, par * D:(par + 1) * D])

        def load_pair_w(pr):
            wq_p, wk_p, wo_p = wviews(pr % 2)
            for c in range(NC):
                ldrow(wq_p[:, c * 128:(c + 1) * 128],
                      din["wqT"][c * 128:(c + 1) * 128,
                                 pr * 128:(pr + 1) * 128])
            for lc in range(2):
                ldrow(wk_p[:, lc * 128:(lc + 1) * 128],
                      din["wkbT"][lc * 128:(lc + 1) * 128,
                                  pr * 128:(pr + 1) * 128])
            ldrow(wo_p, din["woT"][pr * 128:(pr + 1) * 128, :])

        for half in range(_nhalf if _p3 else 0):
            hbase = half * HALF
            if "noxnl" not in _bs:
                for c in range(NC):
                    ldrow(xnh[:, c * HALF:(c + 1) * HALF],
                          scxn[c * 128:(c + 1) * 128, hbase:hbase + HALF])
            load_pair_w(0)
            p.barrier()
            roped_v = {0: 0, 1: 0}
            for pair in range(0 if "p4only" in _bs else _npair):
                hA, hB = 2 * pair, 2 * pair + 1
                par = pair % 2
                wq_p, wk_p, wo_p = wviews(par)
                pcut = float(os.environ.get("MK_PAIRCUT", "99"))
                if pair == 0:
                    if "norope" not in _bs:
                        rope_cols(wq_p, NC, pair)
                        rope_cols(wk_p, 2, pair)
                    rope_done = p.val("sVE")
                else:
                    rope_done = roped_v[par]
                if pcut < 2:
                    p.barrier()
                    continue
                p.wait("tensor", "sVE", rope_done)
                qkv_ev = []
                for blk in range(NBH):
                    bank = pb[len(qkv_ev) % 4]
                    if len(qkv_ev) >= 4:
                        p.wait("tensor", "sVE", qkv_ev[-4])
                    for c in range(NC):
                        p.op("tensor",
                             lambda e, blk=blk, c=c, bank=bank, wq_p=wq_p:
                             nc.tensor.matmul(
                                 bank[:, :], wq_p[:, c * 128:(c + 1) * 128],
                                 xnh[:, c * HALF + blk * TQB: c * HALF + blk * TQB + TQB],
                                 start=(c == 0), stop=(c == NC - 1)), inc="sPE")
                    p.wait("vector", "sPE")
                    qkv_ev.append(p.op(
                        "vector", lambda e, blk=blk, bank=bank: e.tensor_copy(
                            q_h[:, blk * TQB:(blk + 1) * TQB], bank[:, :]),
                        inc="sVE"))
                for blk in range(NB):
                    bank = pb[len(qkv_ev) % 4]
                    if len(qkv_ev) >= 4:
                        p.wait("tensor", "sVE", qkv_ev[-4])
                    for lc in range(2):
                        p.op("tensor",
                             lambda e, blk=blk, lc=lc, bank=bank, wk_p=wk_p:
                             nc.tensor.matmul(
                                 bank[:, :], wk_p[:, lc * 128:(lc + 1) * 128],
                                 kcsb[:, lc * T + blk * TQB: lc * T + blk * TQB + TQB],
                                 start=(lc == 0), stop=(lc == 1)), inc="sPE")
                    p.wait("vector", "sPE")
                    qkv_ev.append(p.op(
                        "vector", lambda e, blk=blk, bank=bank: e.tensor_copy(
                            k_pair[:, blk * TQB:(blk + 1) * TQB], bank[:, :]),
                        inc="sVE"))
                V_ev = [0] * NTK
                voff = len(qkv_ev)
                for tc in range(NTK):
                    bank = pb[(voff + tc) % 4]
                    if tc >= 4:
                        p.wait("tensor", "sVE", V_ev[tc - 4])
                    elif voff + tc >= 4:
                        p.wait("tensor", "sVE", qkv_ev[voff + tc - 4])
                    vv_mm = 0
                    for lc in range(2):
                        vv_mm = p.op(
                            "tensor", lambda e, tc=tc, lc=lc, pair=pair, bank=bank:
                            nc.tensor.matmul(
                                bank[:, 0:128],
                                kvsb[:, lc * T + tc * 128: lc * T + (tc + 1) * 128],
                                wvbT_sb[:, lc * D + pair * 128:
                                        lc * D + (pair + 1) * 128],
                                start=(lc == 0), stop=(lc == 1)), inc="sPE")
                    p.wait("vector", "sPE", vv_mm)
                    p.op("vector", lambda e, tc=tc, bank=bank: e.tensor_copy(
                        v_pad[:, tc * 65: tc * 65 + 64], bank[:, 0:64]))
                    V_ev[tc] = p.op("vector", lambda e, tc=tc, bank=bank:
                                    e.tensor_copy(
                        v_pad[:, 1040 + tc * 65: 1040 + tc * 65 + 64],
                        bank[:, 64:128]), inc="sVE")
                p.wait("tensor", "sVE", V_ev[NTK - 1])
                if pair == 0 and half == 0:
                    tap("q_pair0", q_h[:, 0:HALF], (128, HALF))
                    tap("k_pair0", k_pair[:, 0:T], (128, T))
                if pcut < 3:
                    p.barrier()
                    continue
                for (src, col, nb_) in ((q_h, 0, NBH), (k_pair, 1, NB)):
                    for blk in range(nb_):
                        vv = p.op("vector", lambda e, src=src, blk=blk: e.tensor_mul(
                            tts[:, 0:TQB], src[:, blk * TQB:(blk + 1) * TQB],
                            src[:, blk * TQB:(blk + 1) * TQB]), inc="sVE")
                        p.wait("tensor", "sVE", vv)
                        pv = p.op("tensor", lambda e: nc.tensor.matmul(
                            pb[2][0:2, :], e2sb, tts[:, 0:TQB],
                            start=True, stop=True), inc="sPE")
                        p.wait("vector", "sPE", pv)
                        p.op("vector", lambda e, col=col, blk=blk: e.reduce_max(
                            nrm[0:2, col * 4 + blk: col * 4 + blk + 1],
                            pb[2][0:2, :], axis=AX.X), inc="sVE")
                p.op("vector", lambda e: e.reduce_max(
                    nrm[0:2, 8:9], nrm[0:2, 0:NBH], axis=AX.X))
                p.op("vector", lambda e: e.reduce_max(
                    nrm[0:2, 9:10], nrm[0:2, 4:8], axis=AX.X))
                vspace()
                vv = p.op("vector", lambda e: e.tensor_mul(
                    nrm[0:2, 10:11], nrm[0:2, 8:9], nrm[0:2, 9:10]), inc="sVE")
                p.wait("scalar", "sVE", vv)
                sv = p.op("scalar", lambda e: e.activation(
                    nrm[0:2, 11:12], nrm[0:2, 10:11], AF.Sqrt), inc="sSE")
                p.wait("vector", "sSE", sv)
                vv = p.op("vector", lambda e: e.tensor_scalar(
                    nrm[0:2, 12:13], nrm[0:2, 11:12], -0.125, None, ALU.mult),
                    inc="sVE")
                p.wait("tensor", "sVE", vv)
                pv = p.op("tensor", lambda e: nc.tensor.transpose(
                    pb[2][0:1, 0:2], nrm[0:2, 12:13], ident[0:2, 0:2]), inc="sPE")
                p.wait("vector", "sPE", pv)
                nbrow = rowA[0:1, 3584:3586]
                vv = p.op("vector", lambda e, nbrow=nbrow: e.tensor_copy(
                    nbrow, pb[2][0:1, 0:2]), inc="sVE")
                p.wait("tensor", "sVE", vv)
                pv = p.op("tensor", lambda e, nbrow=nbrow: nc.tensor.matmul(
                    pb[3][:, 0:2], ones_row, nbrow,
                    start=True, stop=True), inc="sPE")
                p.wait("vector", "sPE", pv)
                vspace()
                p.op("vector", lambda e: e.tensor_copy(negb[:, 0:2], pb[3][:, 0:2]),
                     inc="sVE")
                p.wait("tensor", "sVE")
                if pair + 1 < _npair and "p4only" not in _bs:
                    load_pair_w(pair + 1)
                    p.wait("vector", "sLD")
                    if "norope" not in _bs:
                        wq_n, wk_n, _ = wviews((pair + 1) % 2)
                        rope_cols(wq_n, NC, pair + 1)
                        rope_cols(wk_n, 2, pair + 1)
                    roped_v[(pair + 1) % 2] = p.val("sVE")
                if pcut < 4:
                    p.barrier()
                    continue
                for hl, head in ((0, hA), (1, hB)):
                    rows = slice(hl * 64, (hl + 1) * 64)
                    vbase = hl * 1040
                    if pair == 0 and hl == 0 and half == 0:
                        tap("v_pad0", v_pad[:, 0:1040], (128, 1040))
                    for blk in range(NBH):
                        avbank = pb[6 + (blk % 2)]
                        E_done = [0] * NTK
                        A_done = [0] * NTK
                        def av_mm(tc, avbank=avbank, vbase=vbase):
                            p.wait("tensor", "sSE", E_done[tc])
                            A_done[tc] = p.op(
                                "tensor", lambda e, tc=tc, avbank=avbank, vbase=vbase:
                                nc.tensor.matmul(
                                    avbank[0:65, :],
                                    v_pad[:, vbase + tc * 65: vbase + tc * 65 + 65],
                                    attnT[:, (tc % 2) * TQB:(tc % 2 + 1) * TQB],
                                    start=(tc == 0), stop=(tc == NTK - 1)),
                                inc="sPE")
                        for tc in range(NTK):
                            scb = pb[4 + (tc % 2)]
                            if tc >= 2:
                                p.wait("tensor", "sSE", E_done[tc - 2])
                            sc_v = p.op(
                                "tensor",
                                lambda e, tc=tc, blk=blk, rows=rows, scb=scb:
                                nc.tensor.matmul(
                                    scb[:, :],
                                    k_pair[rows, tc * 128:(tc + 1) * 128],
                                    q_h[rows, blk * TQB:(blk + 1) * TQB],
                                    start=True, stop=True), inc="sPE")
                            if tc >= 1:
                                av_mm(tc - 1)
                            p.wait("scalar", "sPE", sc_v)
                            if tc >= 2:
                                p.wait("scalar", "sPE", A_done[tc - 2])
                            E_done[tc] = p.op(
                                "scalar", lambda e, tc=tc, hl=hl, scb=scb:
                                e.activation(
                                    attnT[:, (tc % 2) * TQB:(tc % 2 + 1) * TQB],
                                    scb[:, :], AF.Exp, bias=negb[:, hl:hl + 1],
                                    scale=0.125), inc="sSE")
                        av_mm(NTK - 1)
                        p.wait("vector", "sPE", A_done[NTK - 1])
                        rs_row = rowA[0:1, 3072:3072 + TQB]
                        vspace()
                        vv = p.op("vector", lambda e, avbank=avbank: e.reciprocal(
                            rs_row, avbank[64:65, :]), inc="sVE")
                        p.wait("tensor", "sVE", vv)
                        pv = p.op("tensor", lambda e: nc.tensor.matmul(
                            pb[2][0:64, :], ones_row[0:1, 0:64], rs_row,
                            start=True, stop=True), inc="sPE")
                        p.wait("vector", "sPE", pv)
                        vspace()
                        p.op("vector", lambda e: e.tensor_copy(rs_bc, pb[2][0:64, :]))
                        vspace()
                        if hl == 0:
                            vv = p.op("vector", lambda e, blk=blk, avbank=avbank:
                                      e.tensor_mul(
                                          oT_h[0:64, blk * TQB:(blk + 1) * TQB],
                                          avbank[0:64, :], rs_bc), inc="sVE")
                            p.wait("tensor", "sVE", vv)
                        else:
                            vv = p.op("vector", lambda e, avbank=avbank: e.tensor_mul(
                                tmpB, avbank[0:64, :], rs_bc), inc="sVE")
                            p.wait("tensor", "sVE", vv)
                            pv = p.op("tensor", lambda e: nc.tensor.matmul(
                                pb[3][:, :], shift64, tmpB,
                                start=True, stop=True), inc="sPE")
                            p.wait("vector", "sPE", pv)
                            vspace()
                            p.op("vector", lambda e, blk=blk: e.tensor_copy(
                                oT_h[64:128, blk * TQB:(blk + 1) * TQB],
                                pb[3][64:128, :]), inc="sVE")
                            p.wait("tensor", "sVE")
                if pair == 0 and half == 0:
                    tap("oT_pair0", oT_h[:, 0:HALF], (128, HALF))
                if pcut < 5:
                    p.barrier()
                    continue
                wo_ev = [0] * (NC * NBH)
                for i, (dm, blk) in enumerate(
                        (dm, blk) for dm in range(NC) for blk in range(NBH)):
                    bank = pb[i % 4]
                    if i >= 4:
                        p.wait("tensor", "sVE", wo_ev[i - 4])
                    pv = p.op("tensor", lambda e, dm=dm, blk=blk, bank=bank,
                              wo_p=wo_p:
                              nc.tensor.matmul(
                                  bank[:, :], wo_p[:, dm * 128:(dm + 1) * 128],
                                  oT_h[:, blk * TQB:(blk + 1) * TQB],
                                  start=True, stop=True), inc="sPE")
                    p.wait("vector", "sPE", pv)
                    dst = aoh[:, dm * HALF + blk * TQB:
                              dm * HALF + blk * TQB + TQB]
                    if pair == 0:
                        wo_ev[i] = p.op("vector", lambda e, dst=dst, bank=bank:
                                        e.tensor_copy(dst, bank[:, :]), inc="sVE")
                    else:
                        wo_ev[i] = p.op("vector", lambda e, dst=dst, bank=bank:
                                        e.tensor_add(dst, dst, bank[:, :]),
                                        inc="sVE")
                p.barrier()
            if half == 0:
                tap("ao", aoh[:, 0:NC * HALF], (128, NC * HALF))
            xrl = tts
            for blk in range(min(NBH, int(os.environ.get('MK_P4NB', '9'))) if _p4 else 0):
                base = hbase + blk * TQB

                def aoc(c, blk=blk):
                    src_t = xnh if "p4xn" in _bs else aoh
                    return src_t[:, c * HALF + blk * TQB: c * HALF + blk * TQB + TQB]
                vv = ln_stats(aoc, TQB)
                p.wait("sync", "sSE")
                p.wait("sync", "sPE")
                ld_v = {0: 0, 1: 0}
                st_v = {0: 0, 1: 0}
                for c in range(0 if "p4ln" in _bs else NC):
                    par = c % 2
                    lsem = "sLD" if par == 0 else "sLDb"
                    ssem = "sST" if par == 0 else "sSTb"
                    buf = sqs[:, par * 512: par * 512 + TQB]
                    if ld_v[par]:
                        p.wait("sync", lsem, ld_v[par])
                    if st_v[par]:
                        p.wait("sync", ssem, st_v[par])
                    ld_v[par] = ldrow(buf, scx[c * 128:(c + 1) * 128,
                                               base:base + TQB], sem=lsem)
                    p.wait("vector", "sVE", vv)
                    p.op("vector", lambda e, c=c, aoc=aoc: e.tensor_mul(
                        xrl[:, 0:TQB], aoc(c), absb[:, 0:TQB]))
                    p.op("vector", lambda e: e.tensor_add(
                        xrl[:, 0:TQB], xrl[:, 0:TQB], absb[:, 512:1024]))
                    p.wait("vector", lsem, ld_v[par])
                    wv = p.op("vector", lambda e, buf=buf: e.tensor_add(
                        buf, buf, xrl[:, 0:TQB]), inc="sVE")
                    p.wait("gpsimd", "sVE", wv)
                    if st_v[par]:
                        p.wait("gpsimd", ssem, st_v[par])
                    st_v[par] = p.op("gpsimd", lambda e, c=c, base=base, buf=buf:
                                     e.dma_start(
                                         out=scx2[c * 128:(c + 1) * 128,
                                                  base:base + TQB],
                                         in_=buf), inc=ssem, n=16)
                p.barrier()
        A.release()
        A.release()
        tap("x2", sqs[:, 512:1024], (128, 512))

        w1q = A.alloc(BF16, 128, NC * HID)
        w2q = A.alloc(BF16, 128, 32 * D)
        x2c = A.alloc(F32, 128, 2 * NC * 130)
        wld = x2c[0:128, 0:1024]
        xq = A.alloc(BF16, 128, NC * 130)
        hbuf = A.alloc(F32, 128, 2 * 130)
        ysc = A.alloc(F32, 128, 130)
        s_sb = A.alloc(F32, 128, 32 * FT)
        sq_sb = A.alloc(BF16, 128, 32 * FT)
        ffnb = A.alloc(F32, 128, NC * FT)
        x3b = A.alloc(F32, 128, FT)
        assert A.hi <= 210432, f"arena overflow: {A.hi}"

        def quant_weights(wT, nchunks, ccols, qdst, statcol):
            nq = ccols // 512
            ld_v = {0: 0, 1: 0}
            use_v = {0: 0, 1: 0}
            def wpiece(idx, c, qq):
                par = idx % 2
                lsem = "sLD" if par == 0 else "sLDb"
                buf = wld[:, par * 512:(par + 1) * 512]
                if ld_v[par]:
                    p.wait("sync", lsem, ld_v[par])
                if use_v[par]:
                    p.wait("sync", "sVE", use_v[par])
                ld_v[par] = ldrow(buf, wT[c * 128:(c + 1) * 128,
                                          qq * 512:(qq + 1) * 512], sem=lsem)
                p.wait("vector", lsem, ld_v[par])
                return buf
            npieces = nchunks * nq
            assert npieces <= 64
            for idx, (c, qq) in enumerate(
                    (c, qq) for c in range(nchunks) for qq in range(nq)):
                buf = wpiece(idx, c, qq)
                use_v[idx % 2] = p.op("vector", lambda e, buf=buf, idx=idx:
                                      e.reduce_sum(
                    tts[:, 64 + idx:65 + idx], buf, axis=AX.X,
                    apply_absolute_value=True), inc="sVE")
            vspace()
            vspace()
            p.op("vector", lambda e, n=npieces: e.reduce_sum(
                wstat[:, 1:2], tts[:, 64:64 + n], axis=AX.X))
            vspace()
            vv = p.marker("vector", "sVE")
            p.wait("tensor", "sVE", vv)
            pv = p.op("tensor", lambda e: nc.tensor.matmul(
                pb[0][0:1, 0:1], wstat[:, 1:2], ones_col,
                start=True, stop=True), inc="sPE")
            p.wait("vector", "sPE", pv)
            p.op("vector", lambda e, n=nchunks * 128 * ccols: e.tensor_scalar(
                wrow[0:1, statcol:statcol + 1], pb[0][0:1, 0:1], 1.0 / n, 1e-5,
                ALU.mult, ALU.max))
            vspace()
            vv = p.marker("vector", "sVE")
            p.wait("tensor", "sVE", vv)
            pv = p.op("tensor", lambda e: nc.tensor.matmul(
                pb[1][:, 0:1], ones_row, wrow[0:1, statcol:statcol + 1],
                start=True, stop=True), inc="sPE")
            p.wait("vector", "sPE", pv)
            p.op("vector", lambda e: e.tensor_copy(wstat[:, 4:5], pb[1][:, 0:1]))
            vspace()
            vspace()
            p.op("vector", lambda e: e.reciprocal(wstat[:, 2:3], wstat[:, 4:5]))
            vspace()
            vspace()
            for idx, (c, qq) in enumerate(
                    (c, qq) for c in range(nchunks) for qq in range(nq)):
                buf = wpiece(idx, c, qq)
                p.op("vector", lambda e, buf=buf: e.tensor_scalar(
                    buf, buf, wstat[:, 2:3], None, ALU.mult))
                p.op("vector", lambda e, buf=buf: e.tensor_scalar(
                    buf, buf, R23, None, ALU.add))
                p.op("vector", lambda e, buf=buf: e.tensor_scalar(
                    buf, buf, -R23, None, ALU.add))
                use_v[idx % 2] = p.op("vector", lambda e, c=c, qq=qq, buf=buf:
                                      e.tensor_scalar(
                    qdst[:, c * ccols + qq * 512: c * ccols + (qq + 1) * 512],
                    buf, -1.0, 1.0, ALU.max, ALU.min), inc="sVE")
        if _p5:
            quant_weights(din["w1T"], NC, HID, w1q, 0)
            tap("wst", wstat[:, 0:4], (128, 4))
            tap("wrr", wrow[0:1, 0:8], (1, 8))
            p.barrier()
            quant_weights(din["w2T"], 32, D, w2q, 4)
            p.barrier()
        tap("w1q", w1q[:, 0:2048].bitcast(F32), (128, 1024))
        tap("w2q", w2q[:, 0:2048].bitcast(F32), (128, 1024))

        for fc in range(_nfc if _p5 else 0):
            t0 = fc * FT
            lo = max(t0 - 1, 0)
            hi = min(t0 + FT + 1, T)
            npos = hi - lo
            pofs = lo - (t0 - 1)
            xo = (fc % 2) * (NC * 130)
            if fc == 0:
                for c in range(NC):
                    ldrow(x2c[:, c * 130: c * 130 + npos],
                          scx2[c * 128:(c + 1) * 128, lo:lo + npos])
                p.barrier()
            if fc + 1 < _nfc:
                t0n = (fc + 1) * FT
                lon = max(t0n - 1, 0)
                nposn = min(t0n + FT + 1, T) - lon
                xon = ((fc + 1) % 2) * (NC * 130)
                for c in range(NC):
                    ldrow(x2c[:, xon + c * 130: xon + c * 130 + nposn],
                          scx2[c * 128:(c + 1) * 128, lon:lon + nposn])

            def x2ch(c, npos=npos, xo=xo):
                return x2c[:, xo + c * 130: xo + c * 130 + npos]
            if fc == 0:
                tap("x2c0", x2c[:, 0:NC * 130], (128, NC * 130))
            vv = ln_stats(x2ch, npos)

            def xn2c(c, npos=npos):
                return s_sb[:, c * 130: c * 130 + npos]
            p.wait("vector", "sVE", vv)
            for c in range(NC):
                p.op("vector", lambda e, c=c, npos=npos, x2ch=x2ch: e.tensor_mul(
                    tts[:, 0:npos], x2ch(c), absb[:, 0:npos]))
                p.op("vector", lambda e, c=c, npos=npos, xn2c=xn2c: e.tensor_add(
                    xn2c(c), tts[:, 0:npos], absb[:, 512:512 + npos]), inc="sVE")
            xv = s_sb[:, 0:NC * 130].rearrange(
                "p (c t) -> p t c", c=NC, t=130)[:, 0:npos, :]
            p.op("vector", lambda e, xv=xv, npos=npos: e.reduce_max(
                sqs[:, 0:npos], xv, axis=AX.X, apply_absolute_value=True))
            vv = p.marker("vector", "sVE")
            p.wait("tensor", "sVE", vv)
            ncols = [128, npos - 128] if npos > 128 else [npos]
            rv = 0
            for i, w in enumerate(ncols):
                if i:
                    p.wait("tensor", "sVE")
                pv = p.op("tensor", lambda e, i=i, w=w: nc.tensor.transpose(
                    pb[4][0:w, 0:128], sqs[:, i * 128: i * 128 + w], ident),
                    inc="sPE")
                p.wait("vector", "sPE", pv)
                rv = p.op("vector", lambda e, i=i, w=w: e.reduce_max(
                    tts[0:w, i: i + 1], pb[4][0:w, 0:128], axis=AX.X), inc="sVE")
            p.wait("tensor", "sVE", rv)
            for i, w in enumerate(ncols):
                pv = p.op("tensor", lambda e, i=i, w=w: nc.tensor.transpose(
                    pb[5][0:1, i * 128: i * 128 + w], tts[0:w, i:i + 1],
                    ident[0:w, 0:w]), inc="sPE")
            p.wait("vector", "sPE", pv)
            m1c = rowB[0:1, 1024:1024 + npos]
            s1r = rowA[0:1, 3072:3072 + npos]
            g1r = rowA[0:1, 3328:3328 + npos]
            vspace()
            p.op("vector", lambda e, npos=npos, m1c=m1c: e.tensor_scalar(
                m1c, pb[5][0:1, 0:npos], 1e-5, None, ALU.max))
            vspace()
            p.op("vector", lambda e, s1r=s1r, m1c=m1c: e.reciprocal(s1r, m1c))
            vspace()
            p.op("vector", lambda e, s1r=s1r: e.tensor_scalar(
                s1r, s1r, 127.0, None, ALU.mult))
            vspace()
            vv = p.op("vector", lambda e, g1r=g1r, m1c=m1c: e.tensor_scalar(
                g1r, m1c, 1.0 / 127.0, wrow[0:1, 0:1], ALU.mult, ALU.mult),
                inc="sVE")
            p.wait("tensor", "sVE", vv)
            p.op("tensor", lambda e, npos=npos, s1r=s1r: nc.tensor.matmul(
                pb[2][:, 0:npos], ones_row, s1r, start=True, stop=True), inc="sPE")
            pv = p.op("tensor", lambda e, npos=npos, g1r=g1r: nc.tensor.matmul(
                pb[3][:, 0:npos], ones_row, g1r, start=True, stop=True), inc="sPE")
            p.wait("vector", "sPE", pv)
            s1bc = absb[:, 0:npos]
            g1bc = absb[:, 512:512 + npos]
            vspace()
            p.op("vector", lambda e, npos=npos, s1bc=s1bc: e.tensor_copy(
                s1bc, pb[2][:, 0:npos]))
            vspace()
            p.op("vector", lambda e, npos=npos, g1bc=g1bc: e.tensor_copy(
                g1bc, pb[3][:, 0:npos]), inc="sVE")
            for c in range(NC):
                p.op("vector", lambda e, c=c, npos=npos, xn2c=xn2c, s1bc=s1bc:
                     e.tensor_mul(tts[:, 0:npos], xn2c(c), s1bc))
                p.op("vector", lambda e, npos=npos: e.tensor_scalar(
                    tts[:, 0:npos], tts[:, 0:npos], R23, None, ALU.add))
                p.op("vector", lambda e, c=c, npos=npos: e.tensor_scalar(
                    xq[:, c * 130: c * 130 + npos], tts[:, 0:npos],
                    -R23, None, ALU.add), inc="sVE")
            p.wait("tensor", "sVE")
            if fc == 0:
                tap("xq0", xq[:, 0:NC * 130].bitcast(F32), (128, NC * 65))
            mul_v = [0] * 32
            id_v = [0] * 32
            for hc in range(32):
                bank = pb[hc % 4]
                if hc >= 4:
                    p.wait("tensor", "sVE", mul_v[hc - 4])
                for c in range(NC):
                    p.op("tensor", lambda e, hc=hc, c=c, npos=npos, bank=bank:
                         nc.tensor.matmul(
                             bank[:, 0:npos],
                             w1q[:, c * HID + hc * 128: c * HID + (hc + 1) * 128],
                             xq[:, c * 130: c * 130 + npos],
                             start=(c == 0), stop=(c == NC - 1)), inc="sPE")
                p.wait("vector", "sPE")
                hb = hbuf[:, (hc % 2) * 130:(hc % 2) * 130 + 130]
                if hc >= 2:
                    p.wait("vector", "sSE", id_v[hc - 2])
                if pofs == 1:
                    p.op("vector", lambda e, hb=hb: e.memset(hb[:, 0:1], 0.0))
                if hi == T:
                    p.op("vector", lambda e, hb=hb: e.memset(hb[:, 129:130], 0.0))
                mul_v[hc] = p.op(
                    "vector",
                    lambda e, hb=hb, pofs=pofs, npos=npos, bank=bank, g1bc=g1bc:
                    e.tensor_mul(hb[:, pofs:pofs + npos], bank[:, 0:npos],
                                 g1bc), inc="sVE")
                if fc == 0 and hc == 0:
                    tap("h0", hb[:, 0:130], (128, 130))
                p.wait("scalar", "sVE", mul_v[hc])
                id_v[hc] = p.op("scalar", lambda e, hc=hc, hb=hb: e.activation(
                    ysc[:, 0:FT], hb[:, 1:1 + FT], AF.Identity,
                    bias=convb_sb[:, hc:hc + 1],
                    scale=convw_sb[:, hc * 3 + 1: hc * 3 + 2]), inc="sSE")
                p.wait("vector", "sSE", id_v[hc])
                p.op("vector", lambda e, hc=hc, hb=hb: e.scalar_tensor_tensor(
                    ysc[:, 0:FT], hb[:, 0:FT], convw_sb[:, hc * 3: hc * 3 + 1],
                    ysc[:, 0:FT], ALU.mult, ALU.add))
                vv = p.op("vector", lambda e, hc=hc, hb=hb: e.scalar_tensor_tensor(
                    ysc[:, 0:FT], hb[:, 2:2 + FT],
                    convw_sb[:, hc * 3 + 2: hc * 3 + 3],
                    ysc[:, 0:FT], ALU.mult, ALU.add), inc="sVE")
                p.wait("scalar", "sVE", vv)
                p.op("scalar", lambda e, hc=hc: e.activation(
                    s_sb[:, hc * FT:(hc + 1) * FT], ysc[:, 0:FT], AF.Silu),
                    inc="sSE")
            p.wait("vector", "sSE")
            if fc == 0:
                tap("s0", s_sb[:, 0:32 * FT], (128, 32 * FT))
            sv_ = s_sb[:, 0:32 * FT].rearrange("p (h t) -> p t h", h=32, t=FT)
            p.op("vector", lambda e, sv_=sv_: e.reduce_max(
                sqs[:, 0:FT], sv_, axis=AX.X, apply_absolute_value=True))
            vv = p.marker("vector", "sVE")
            p.wait("tensor", "sVE", vv)
            pv = p.op("tensor", lambda e: nc.tensor.transpose(
                pb[4][0:FT, 0:128], sqs[:, 0:FT], ident), inc="sPE")
            p.wait("vector", "sPE", pv)
            vv = p.op("vector", lambda e: e.reduce_max(
                tts[0:FT, 0:1], pb[4][0:FT, 0:128], axis=AX.X), inc="sVE")
            p.wait("tensor", "sVE", vv)
            pv = p.op("tensor", lambda e: nc.tensor.transpose(
                pb[5][0:1, 0:FT], tts[0:FT, 0:1], ident[0:FT, 0:FT]), inc="sPE")
            p.wait("vector", "sPE", pv)
            m2c = rowB[0:1, 2560:2560 + FT]
            s2r = rowA[0:1, 3072:3072 + FT]
            g2r = rowA[0:1, 3328:3328 + FT]
            vspace()
            p.op("vector", lambda e: e.tensor_scalar(
                m2c, pb[5][0:1, 0:FT], 1e-5, None, ALU.max))
            vspace()
            p.op("vector", lambda e: e.reciprocal(s2r, m2c))
            vspace()
            p.op("vector", lambda e: e.tensor_scalar(
                s2r, s2r, 127.0, None, ALU.mult))
            vspace()
            vv = p.op("vector", lambda e: e.tensor_scalar(
                g2r, m2c, 1.0 / 127.0, wrow[0:1, 4:5], ALU.mult, ALU.mult),
                inc="sVE")
            p.wait("tensor", "sVE", vv)
            p.op("tensor", lambda e: nc.tensor.matmul(
                pb[2][:, 0:FT], ones_row, s2r, start=True, stop=True), inc="sPE")
            pv = p.op("tensor", lambda e: nc.tensor.matmul(
                pb[3][:, 0:FT], ones_row, g2r, start=True, stop=True), inc="sPE")
            p.wait("vector", "sPE", pv)
            s2bc = absb[:, 0:FT]
            g2bc = absb[:, 512:512 + FT]
            vspace()
            p.op("vector", lambda e: e.tensor_copy(s2bc, pb[2][:, 0:FT]))
            vspace()
            p.op("vector", lambda e: e.tensor_copy(g2bc, pb[3][:, 0:FT]),
                 inc="sVE")
            sqv = [0] * 32
            for hc in range(32):
                p.op("vector", lambda e, hc=hc: e.tensor_mul(
                    tts[:, 0:FT], s_sb[:, hc * FT:(hc + 1) * FT], s2bc))
                p.op("vector", lambda e: e.tensor_scalar(
                    tts[:, 0:FT], tts[:, 0:FT], R23, None, ALU.add))
                sqv[hc] = p.op("vector", lambda e, hc=hc: e.tensor_scalar(
                    sq_sb[:, hc * FT:(hc + 1) * FT], tts[:, 0:FT],
                    -R23, None, ALU.add), inc="sVE")
            if fc == 0:
                p.wait("tensor", "sVE")
                tap("sq0", sq_sb[:, 0:4096].bitcast(F32), (128, 2048))
                tap("m2row", rowB[0:1, 2560:2560 + FT], (1, FT))
                tap("g2row", rowA[0:1, 3328:3328 + FT], (1, FT))
            for hc in range(32):
                p.wait("tensor", "sVE", sqv[hc])
                for dm in range(NC):
                    p.op("tensor", lambda e, dm=dm, hc=hc:
                         nc.tensor.matmul(
                             pb[dm][:, 0:FT],
                             w2q[:, hc * D + dm * 128: hc * D + (dm + 1) * 128],
                             sq_sb[:, hc * FT:(hc + 1) * FT],
                             start=(hc == 0), stop=(hc == 31)), inc="sPE")
            p.wait("vector", "sPE")
            for dm in range(NC):
                p.op("vector", lambda e, dm=dm, g2bc=g2bc: e.tensor_mul(
                    ffnb[:, dm * FT:(dm + 1) * FT], pb[dm][:, 0:FT], g2bc),
                    inc="sVE")

            if fc == 0:
                tap("ffnb0", ffnb[:, 0:NC * FT], (128, NC * FT))
            if fc == 1:
                tap("ffnb1", ffnb[:, 0:NC * FT], (128, NC * FT))

            def fch(c):
                return ffnb[:, c * FT:(c + 1) * FT]
            p.wait("scalar", "sVE")
            p.wait("tensor", "sVE")
            vv = ln_stats(fch, FT)
            p.wait("vector", "sVE", vv)
            yst_v = {0: 0, 1: 0}
            for c in range(NC):
                par = c % 2
                ssem = "sST" if par == 0 else "sSTb"
                buf = x3b[:, 0:FT] if par == 0 else ysc[:, 0:FT]
                p.op("vector", lambda e, c=c: e.tensor_mul(
                    tts[:, 0:FT], fch(c), absb[:, 0:FT]))
                p.op("vector", lambda e: e.tensor_add(
                    tts[:, 0:FT], tts[:, 0:FT], absb[:, 512:512 + FT]))
                if yst_v[par]:
                    p.wait("vector", ssem, yst_v[par])
                x2o = t0 - lo
                wv = p.op("vector", lambda e, c=c, x2o=x2o, buf=buf, xo=xo:
                          e.tensor_add(
                    buf, x2c[:, xo + c * 130 + x2o: xo + c * 130 + x2o + FT],
                    tts[:, 0:FT]), inc="sVE")
                p.wait("gpsimd", "sVE", wv)
                if yst_v[par]:
                    p.wait("gpsimd", ssem, yst_v[par])
                yst_v[par] = p.op("gpsimd", lambda e, c=c, t0=t0, buf=buf:
                                  e.dma_start(
                                      out=y[c * 128:(c + 1) * 128, t0:t0 + FT],
                                      in_=buf), inc=ssem, n=16)
            p.barrier()

        p.wait("tensor", "sVE")
        p.op("tensor", lambda e: nc.tensor.matmul(
            pbT[0][0:1, 0:128], ones_col, ident,
            start=True, stop=True), inc="sPE")
        p.wait("gpsimd", "sPE")
        if "delay" in _bs:
            for i in range(int(os.environ.get("MK_DELAY", "2000"))):
                p.op("vector", lambda e: e.tensor_copy(rowB[0:1, 2752:2760],
                                                       rowB[0:1, 2744:2752]))
            if "nodelaywait" not in _bs:
                p.marker("vector", "sVE")
                for eng in ENGINES:
                    p.wait(eng, "sVE")
        p.wait("gpsimd", "sST")
        p.wait("gpsimd", "sLD")

        with nc.Block() as block:
            @block.sync
            def _(e):
                for f in p.streams["sync"]:
                    f(e)

            @block.tensor
            def _(e):
                for f in p.streams["tensor"]:
                    f(e)

            @block.vector
            def _(e):
                for f in p.streams["vector"]:
                    f(e)

            @block.scalar
            def _(e):
                for f in p.streams["scalar"]:
                    f(e)

            @block.gpsimd
            def _(e):
                for f in p.streams["gpsimd"]:
                    f(e)

    return nc, dbg_out


def make_inputs(inputs):
    f32 = np.float32
    C = np.ascontiguousarray
    x = np.asarray(inputs["L0_attn_x"], f32)
    shared = {
        "wqT": C(np.asarray(inputs["L0_attn_wq"], f32).T),
        "wkaT": C(np.asarray(inputs["L0_attn_wk_a"], f32).T),
        "wkbT": C(np.asarray(inputs["L0_attn_wk_b"], f32).T),
        "wvaT": C(np.asarray(inputs["L0_attn_wv_a"], f32).T),
        "wvbT": C(np.asarray(inputs["L0_attn_wv_b"], f32).T),
        "woT": C(np.asarray(inputs["L0_attn_wo"], f32).T),
        "w1T": C(np.asarray(inputs["L0_ffn_w1"], f32).T),
        "w2T": C(np.asarray(inputs["L0_ffn_w2"], f32).T),
        "convw": C(np.asarray(inputs["L0_ffn_conv_w"], f32).reshape(HID, 3)),
        "convb": C(np.asarray(inputs["L0_ffn_conv_b"], f32).reshape(HID, 1)),
    }
    inv = (f32(1.0) / np.power(f32(10000.0),
                               np.arange(0, HD, 2, dtype=f32) / f32(HD))).astype(f32)
    ang = (np.arange(T, dtype=f32)[:, None] * inv[None, :]).astype(f32)
    cosf = np.cos(ang).astype(f32)[:NH, :8]
    sinf = np.sin(ang).astype(f32)[:NH, :8]
    ropeC = np.zeros((8, 8, 2, 8), f32)
    ropeS = np.zeros((8, 8, 2, 8), f32)
    for pr in range(8):
        for c in range(8):
            for j in range(2):
                ropeC[pr, c, j, :] = cosf[2 * pr + j]
                ropeS[pr, c, j, :] = sinf[2 * pr + j]
    e2 = np.zeros((128, 2), f32)
    e2[0:64, 0] = 1.0
    e2[64:128, 1] = 1.0
    sh = np.zeros((64, 128), f32)
    sh[np.arange(64), 64 + np.arange(64)] = 1.0
    shared.update({
        "ropeC": np.broadcast_to(ropeC.reshape(1, 1024), (128, 1024)).copy(),
        "ropeS": np.broadcast_to(ropeS.reshape(1, 1024), (128, 1024)).copy(),
        "ones_col": np.ones((128, 1), f32), "ones_row": np.ones((1, 128), f32),
        "e2": e2, "shift64": sh, "ident": np.eye(128, dtype=f32),
    })
    in_maps = []
    for core in range(8):
        m = dict(shared)
        m["xT"] = C(x[core].T)
        in_maps.append(m)
    return in_maps


def kernel(**inputs):
    in_maps = make_inputs(inputs)
    nc, _ = build_program()
    res = run_bass_kernel_spmd(nc, in_maps, core_ids=list(range(8)))
    out = np.stack([np.ascontiguousarray(r["y"].T) for r in res.results])
    return out.astype(np.float32)
